# version 31
# baseline (speedup 1.0000x reference)
import contextlib
import numpy as np
import concourse.bass as bass
import concourse.mybir as mybir
from concourse.bass_utils import run_bass_kernel_spmd

F32 = mybir.dt.float32
BF16 = mybir.dt.bfloat16
AF = mybir.ActivationFunctionType
ALU = mybir.AluOpType
AX = mybir.AxisListType

B, T, D = 4, 4096, 1024
NCORE = 8
COLS = 2080
NCOMP = 2076
OUT0, OUT1 = 20, 2076
OUTW = OUT1 - OUT0

ATW = 260
NA = COLS // ATW
AW = ATW + 1
CTW = 346
NCT = 6
CW = CTW + 1

SCALE = float(np.float32(32 ** (-0.5)))
NEG = -1.0e30

def _chunks(base, last_start, last_w):
    out = [(base + 120 * k, 120, 0 if k == 0 else 1) for k in range(17)]
    out.append((last_start, last_w, 2))
    return out

CHUNKS = {0: _chunks(15, 2055, 25), 1: _chunks(10, 2050, 30)}
BR_W = {0: 10, 1: 20}
BR_S = {0: 5, 1: 10}
CLS_W = {0: [120, 120, 25], 1: [120, 120, 30]}

NBLK = 128 * 128
NBLOB = (64 * 3 + 32 + 32 + 128 + 128) * NBLK
NPRE = 2 * (8 * NBLK + 2 * 128 * 256 + 4 * NBLK)
NTAB = 8 * (2 * 120 * 120 + 25 * 25) + 8 * (2 * 120 * 120 + 30 * 30)

DEBUG_TAPS = []
PASSES = "ABC"
BSTAGE = 9
BDUMP = -1

VECS = {
    "down_b": (0, 4), "norm1_w": (4, 4), "up_b": (8, 8), "norm2_w": (16, 8),
    "ffn_b1": (24, 16), "ffn_b2": (40, 8), "mu_r": (48, 8), "mu_k": (56, 8),
    "proj1_b": (64, 2), "proj2_b": (66, 2),
    "qkv1_b_qk": (68, 4), "qkv2_b_qk": (72, 4),
}
NV = 78
EPS5_COL, EPS6_COL = 76, 77

OFFS = {
    "Wr": 0, "Wk": 64 * NBLK, "Wv": 128 * NBLK, "down": 192 * NBLK,
    "up": 224 * NBLK, "ffn1": 256 * NBLK, "ffn2": 384 * NBLK,
}


def _pack_vecs(ins):
    v = np.zeros((128, NV), np.float32)

    def put(name, arr):
        c0, n = VECS[name]
        v[:, c0:c0 + n] = np.asarray(arr, np.float32).reshape(n, 128).T

    put("down_b", ins["L0_attn_down_b"])
    put("norm1_w", ins["L0_attn_norm1_w"])
    put("up_b", ins["L0_attn_up_b"])
    put("norm2_w", ins["L0_attn_norm2_w"])
    put("ffn_b1", ins["L0_attn_ffn_b1"])
    put("ffn_b2", ins["L0_attn_ffn_b2"])
    put("mu_r", ins["L0_ffn_mu_r"])
    put("mu_k", ins["L0_ffn_mu_k"])
    put("proj1_b", ins["L0_attn_proj1_b"])
    put("proj2_b", ins["L0_attn_proj2_b"])
    put("qkv1_b_qk", np.asarray(ins["L0_attn_qkv1_b"])[0:512])
    put("qkv2_b_qk", np.asarray(ins["L0_attn_qkv2_b"])[0:512])
    v[:, EPS5_COL] = 1e-5
    v[:, EPS6_COL] = 1e-6
    return v


def _blob(ins):
    parts = []

    def addmat(lhsT, nm, nk):
        for m in range(nm):
            for k in range(nk):
                parts.append(np.ascontiguousarray(
                    lhsT[k * 128:(k + 1) * 128, m * 128:(m + 1) * 128],
                    np.float32).reshape(-1))

    addmat(np.asarray(ins["L0_ffn_W_r"], np.float32).T, 8, 8)
    addmat(np.asarray(ins["L0_ffn_W_k"], np.float32).T, 8, 8)
    addmat(np.asarray(ins["L0_ffn_W_v"], np.float32).T, 8, 8)
    addmat(np.asarray(ins["L0_attn_down_w"], np.float32), 4, 8)
    addmat(np.asarray(ins["L0_attn_up_w"], np.float32), 8, 4)
    addmat(np.asarray(ins["L0_attn_ffn_w1"], np.float32), 16, 8)
    addmat(np.asarray(ins["L0_attn_ffn_w2"], np.float32), 8, 16)
    out = np.concatenate(parts)
    assert out.size == NBLOB, out.size
    return out


def _pre(ins):
    parts = []
    for bi in (0, 1):
        qkv = np.asarray(ins[f"L0_attn_qkv{bi+1}_w"], np.float32)
        pj = np.asarray(ins[f"L0_attn_proj{bi+1}_w"], np.float32)
        for m in range(4):
            for k in range(2):
                parts.append(np.ascontiguousarray(
                    qkv[k * 128:(k + 1) * 128, m * 128:(m + 1) * 128]).reshape(-1))
        for k in range(2):
            parts.append(np.ascontiguousarray(
                qkv[k * 128:(k + 1) * 128, 512:768]).reshape(-1))
        for m in range(2):
            for k in range(2):
                parts.append(np.ascontiguousarray(
                    pj[k * 128:(k + 1) * 128, m * 128:(m + 1) * 128]).reshape(-1))
    out = np.concatenate(parts)
    assert out.size == NPRE, out.size
    return out


def _chunk_table(rpb, W, S, c0, s, C):
    toks = np.arange(s, s + C) + c0
    code = np.full(C, -100, np.int64)
    lo = np.zeros(C, np.int64)
    hi = np.zeros(C, np.int64)
    for i, t in enumerate(toks):
        if t < 0 or t >= 4100:
            continue
        if t < S:
            code[i], lo[i], hi[i] = -2, 0, S
        elif t >= 4100 - W + S:
            code[i], lo[i], hi[i] = -3, 4100 - W + S, 4100
        else:
            k = (t - S) // W
            code[i], lo[i], hi[i] = k, S + W * k, S + W * (k + 1)
    live = (code > -100) & (lo - c0 >= s) & (hi - c0 <= s + C)
    same = (code[:, None] == code[None, :]) & live[:, None] & (code[:, None] > -100)
    dmat = np.clip(toks[:, None] - toks[None, :] + W - 1, 0, 2 * W - 2)
    bias = rpb[dmat, :]
    tab = np.where(same[:, :, None], bias, np.float32(NEG))
    return np.ascontiguousarray(tab.transpose(2, 0, 1).astype(np.float32))


def _tables(ins, half):
    c0 = -20 if half == 0 else 2020
    out = {}
    for bi in (0, 1):
        W, S = BR_W[bi], BR_S[bi]
        rpb = np.asarray(ins[f"L0_attn_rpb{bi+1}"], np.float32)
        for cls, idx in ((0, 0), (1, 1), (2, 17)):
            s, C, _ = CHUNKS[bi][idx]
            out[(bi, cls)] = _chunk_table(rpb, W, S, c0, s, C)
    return out


def _host_pack(ins):
    x = np.asarray(ins["L0_attn_x"], np.float32)
    vecs = _pack_vecs(ins)
    blob = _blob(ins)
    pre = _pre(ins)
    rows = np.concatenate([
        np.asarray(ins["L0_attn_qkv1_b"], np.float32)[512:768],
        np.asarray(ins["L0_attn_qkv2_b"], np.float32)[512:768]])[None, :]
    ident = np.eye(128, dtype=np.float32)
    in_maps = []
    for c in range(NCORE):
        b, h = c // 2, c % 2
        c0 = -20 if h == 0 else 2020
        xs = np.zeros((COLS, D), np.float32)
        lo, hi = max(0, c0), min(T, c0 + COLS)
        xs[lo - c0:hi - c0] = x[b, lo:hi]
        xfm = np.ascontiguousarray(xs.T.reshape(8, 128, COLS))
        tabs = _tables(ins, h)
        atab = np.concatenate([tabs[(bi, cls)].reshape(-1)
                               for bi in (0, 1) for cls in (0, 1, 2)])
        assert atab.size == NTAB
        in_maps.append({"x_in": xfm, "wblob": blob, "wpre": pre, "vecs": vecs,
                        "rows": rows, "ident": ident, "atab": atab})
    return in_maps


def _host_unpack(results):
    out = np.empty((B, T, D), np.float32)
    for c in range(NCORE):
        b, h = c // 2, c % 2
        o = results[c]["out"].reshape(1024, OUTW)
        sl = o[:, 0:2048] if h == 0 else o[:, 8:2056]
        out[b, h * 2048:(h + 1) * 2048, :] = sl.T
    return out


ENGINES = ("sp", "pe", "dv", "ac", "gp")
ENG_SEM = {"pe": "pe", "dv": "dv", "ac": "ac", "gp": "gp"}


class P:
    def __init__(self):
        self.q = {e: [] for e in ENGINES}
        self.cnt = {}
        self.hwm = {e: {} for e in ENGINES}
        self.semh = {}

    def reg_sem(self, name, handle):
        self.semh[name] = handle
        self.cnt[name] = 0
        for e in ENGINES:
            self.hwm[e][name] = 0

    def wait(self, eng, sem, val):
        if val is None or val <= 0:
            return
        if ENG_SEM.get(eng) == sem:
            return
        if self.hwm[eng][sem] >= val:
            return
        self.hwm[eng][sem] = val
        self.q[eng].append(("w", sem, val))

    def op(self, eng, fn, sem=None, n=1):
        self.q[eng].append(("o", fn, sem, n))
        if sem is not None:
            self.cnt[sem] += n
            return self.cnt[sem]
        return None

    def flush(self, eng):
        sem = ENG_SEM[eng]
        self.q[eng].append(("w", sem, self.cnt[sem]))

    def fence(self):
        caps = [(s, self.cnt[s]) for s in ("pe", "dv", "ac", "gp", "ld", "st")]
        for e in ENGINES:
            for s, v in caps:
                self.wait(e, s, v)

    def emit(self, block):
        def mk(items):
            def run(eng):
                for it in items:
                    if it[0] == "w":
                        eng.wait_ge(self.semh[it[1]], it[2])
                    else:
                        inst = it[1](eng)
                        if it[2] is not None:
                            inst.then_inc(self.semh[it[2]], it[3])
            return run
        block.sync(mk(self.q["sp"]))
        block.tensor(mk(self.q["pe"]))
        block.vector(mk(self.q["dv"]))
        block.scalar(mk(self.q["ac"]))
        block.gpsimd(mk(self.q["gp"]))


class Res:
    def __init__(self, p):
        self.p, self.w, self.r = p, None, []

    def rd(self, eng):
        if self.w:
            self.p.wait(eng, *self.w)

    def note_rd(self, sem, cnt):
        self.r.append((sem, cnt))

    def wr(self, eng):
        for s, c in self.r:
            self.p.wait(eng, s, c)
        if self.w:
            self.p.wait(eng, *self.w)
        self.r = []

    def note_wr(self, sem, cnt):
        self.w = (sem, cnt)


class Ring:
    def __init__(self, p, sb, wsplit_h, split_ready, nslots=48):
        self.p, self.sb, self.ws, self.ns = p, sb, wsplit_h, nslots
        self.split_ready = split_ready
        self.cur = 0
        self.slot_note = [None] * nslots
        self.pending = []

    def load_pair_group(self, off_f32, nk):
        nb = 2 * nk
        s0 = self.cur % self.ns
        assert s0 + nb <= self.ns
        gate = self.split_ready(off_f32 + nk * NBLK)
        if gate:
            self.p.wait("sp", *gate)
        seen = set()
        for s in range(s0, s0 + nb):
            note = self.slot_note[s]
            if note and id(note) not in seen:
                seen.add(id(note))
                self.p.wait("sp", *note)
        sb, ws, ns = self.sb, self.ws, self.ns

        def fn(eng, s0=s0, nb=nb, off=2 * off_f32):
            return eng.dma_start(
                bass.AP(sb, s0 * 128, [[ns * 128, 128], [128, nb], [1, 128]]),
                bass.AP(ws, off, [[128, 128], [NBLK, nb], [1, 128]]))
        cnt = self.p.op("sp", fn, "ld", 16)
        self.pending.append((s0, nb, cnt))
        self.cur += nb
        return s0

    def use_group(self):
        s0, nb, cnt = self.pending[0]
        self.p.wait("pe", "ld", cnt)
        return s0

    def done_group(self, pe_cnt):
        s0, nb, _ = self.pending.pop(0)
        note = ("pe", pe_cnt)
        for s in range(s0, s0 + nb):
            self.slot_note[s] = note

    def blk(self, s0, i):
        return self.sb[:, (s0 + i) * 128:(s0 + i + 1) * 128]


def build(debug_taps=()):
    nc = bass.Bass("TRN2", target_bir_lowering=False)
    p = P()

    x_h = nc.dram_tensor("x_in", [8, 128, COLS], F32, kind="ExternalInput")
    blob_h = nc.dram_tensor("wblob", [NBLOB], F32, kind="ExternalInput")
    pre_h = nc.dram_tensor("wpre", [NPRE], F32, kind="ExternalInput")
    vecs_h = nc.dram_tensor("vecs", [128, NV], F32, kind="ExternalInput")
    rows_h = nc.dram_tensor("rows", [1, 512], F32, kind="ExternalInput")
    ident_h = nc.dram_tensor("ident", [128, 128], F32, kind="ExternalInput")
    atab_h = nc.dram_tensor("atab", [NTAB], F32, kind="ExternalInput")
    out_h = nc.dram_tensor("out", [8, 128, OUTW], F32, kind="ExternalOutput")
    vstat_h = nc.dram_tensor("vstat", [2, COLS], F32)
    wsplit_h = nc.dram_tensor("wsplit", [2 * NBLOB], BF16)
    taps = {}
    for t in debug_taps:
        shape = {"y1": [8, 128, COLS], "xi": [4, 128, COLS],
                 "o": [4, 128, COLS]}[t]
        taps[t] = nc.dram_tensor("tap_" + t, shape, F32, kind="ExternalOutput")
    bdump_h = {}
    if BDUMP >= 0:
        for nm_, w_ in (("qk", 512), ("vT", 256), ("S", 1024), ("P", 1024),
                        ("O", 256), ("nmx", 8), ("sms", 8)):
            bdump_h[nm_] = nc.dram_tensor("bd_" + nm_, [128, w_], F32,
                                          kind="ExternalOutput")

    with contextlib.ExitStack() as ctx:
        def sb(name, shape, stack=None, dt=F32):
            return (stack or ctx).enter_context(nc.sbuf_tensor(name, shape, dt))

        y1 = sb("y1", [128, 8 * COLS])
        o_full = sb("o_full", [128, 4 * COLS])
        vecs = sb("vecs_s", [128, NV])
        ones1 = sb("ones1", [1, 128])
        inv1024 = sb("inv1024", [128, 1])
        inv512 = sb("inv512", [128, 1])
        pbk = [ctx.enter_context(nc.psum_tensor(f"pb{i}", [128, 512], F32))
               for i in range(8)]
        pbr = [Res(p) for _ in range(8)]
        for s in ("ld", "st", "pe", "dv", "ac", "gp"):
            p.reg_sem(s, ctx.enter_context(nc.semaphore(s)))

        y1_r, o_r = Res(p), Res(p)
        vecs_r = Res(p)

        def dma(dst_ap, src_ap):
            return lambda e, d=dst_ap, s=src_ap: e.dma_start(d, s)

        ld0 = p.op("sp", dma(vecs[:, :], vecs_h[:, :]), "ld", 16)
        vecs_r.note_wr("ld", ld0)
        p.op("dv", lambda e: e.memset(ones1[:, :], 1.0), "dv")
        p.op("dv", lambda e: e.memset(inv1024[:, :], 1.0 / 1024.0), "dv")
        dv_c = p.op("dv", lambda e: e.memset(inv512[:, :], 1.0 / 512.0), "dv")
        consts_ok = ("dv", dv_c)

        def stats(src, src_r, nch, sstride, scoff, W, inv, eps, want_m,
                  sqs, sq_rs, rA, rB, rC, rows_r, bks=(2, 3, 4, 5),
                  defer_bcast=False):
            b_sum, b_sq, b_rs, b_mrs = bks
            pbr[b_sum].wr("pe")
            pbr[b_sq].wr("pe")
            p.wait("pe", *consts_ok)
            for c in range(nch):
                s_ap = src[:, c * sstride + scoff: c * sstride + scoff + W]
                slot = c % 2
                sq, sq_r = sqs[slot], sq_rs[slot]
                sq_r.wr("ac")
                src_r.rd("ac")
                cnt = p.op("ac", (lambda e, a=s_ap, W=W, sq=sq:
                                  e.activation(sq[:, 0:W], a, AF.Square,
                                               bias=0.0, scale=1.0)), "ac")
                sq_r.note_wr("ac", cnt)
                src_r.note_rd("ac", cnt)
                src_r.rd("pe")
                p.op("pe", (lambda e, a=s_ap, st=(c == 0), sp=(c == nch - 1),
                            W=W, inv=inv, b=b_sum: e.matmul(
                                pbk[b][0:1, 0:W], inv, a, start=st, stop=sp)))
                sq_r.rd("pe")
                pe = p.op("pe", (lambda e, st=(c == 0), sp=(c == nch - 1),
                                 W=W, inv=inv, sq=sq, b=b_sq: e.matmul(
                                     pbk[b][0:1, 0:W], inv, sq[:, 0:W],
                                     start=st, stop=sp)), "pe")
                sq_r.note_rd("pe", pe)
            src_r.note_rd("pe", pe)
            pbr[b_sum].note_wr("pe", pe)
            pbr[b_sq].note_wr("pe", pe)
            rows_r.wr("dv")
            if want_m:
                pbr[b_sum].rd("dv")
                p.op("dv", (lambda e, W=W, b=b_sum: e.tensor_copy(
                    rA[0:1, 0:W], pbk[b][0:1, 0:W])))
                p.op("dv", (lambda e, W=W: e.tensor_tensor(
                    rB[0:1, 0:W], rA[0:1, 0:W], rA[0:1, 0:W], ALU.mult)))
                pbr[b_sq].rd("dv")
                dv = p.op("dv", (lambda e, W=W, b=b_sq: e.tensor_tensor(
                    rC[0:1, 0:W], pbk[b][0:1, 0:W], rB[0:1, 0:W],
                    ALU.subtract)), "dv")
                pbr[b_sum].note_rd("dv", dv)
                pbr[b_sq].note_rd("dv", dv)
                p.wait("ac", "dv", dv)
                ac = p.op("ac", (lambda e, W=W, eps=eps: e.activation(
                    rB[0:1, 0:W], rC[0:1, 0:W], AF.Sqrt,
                    bias=vecs[0:1, eps:eps + 1], scale=1.0)), "ac")
                p.wait("dv", "ac", ac)
                p.op("dv", (lambda e, W=W: e.reciprocal(
                    rB[0:1, 0:W], rB[0:1, 0:W])))
                dv = p.op("dv", (lambda e, W=W: e.tensor_tensor(
                    rC[0:1, 0:W], rA[0:1, 0:W], rB[0:1, 0:W], ALU.mult)), "dv")
                rows_r.note_wr("dv", dv)
            else:
                pbr[b_sq].rd("ac")
                ac = p.op("ac", (lambda e, W=W, eps=eps, b=b_sq: e.activation(
                    rB[0:1, 0:W], pbk[b][0:1, 0:W], AF.Sqrt,
                    bias=vecs[0:1, eps:eps + 1], scale=1.0)), "ac")
                pbr[b_sq].note_rd("ac", ac)
                p.wait("dv", "ac", ac)
                dv = p.op("dv", (lambda e, W=W: e.reciprocal(
                    rB[0:1, 0:W], rB[0:1, 0:W])), "dv")
                rows_r.note_wr("dv", dv)
            def do_bcast():
                pbr[b_rs].wr("pe")
                rows_r.rd("pe")
                pe = p.op("pe", (lambda e, W=W, b=b_rs: e.matmul(
                    pbk[b][:, 0:W], ones1[0:1, :], rB[0:1, 0:W],
                    start=True, stop=True)), "pe")
                pbr[b_rs].note_wr("pe", pe)
                rows_r.note_rd("pe", pe)
                if want_m:
                    pbr[b_mrs].wr("pe")
                    pe = p.op("pe", (lambda e, W=W, b=b_mrs: e.matmul(
                        pbk[b][:, 0:W], ones1[0:1, :], rC[0:1, 0:W],
                        start=True, stop=True)), "pe")
                    pbr[b_mrs].note_wr("pe", pe)
                    rows_r.note_rd("pe", pe)
            if defer_bcast:
                return do_bcast
            do_bcast()

        def normalize(src, src_r, dst, dst_r, nch, sstride, scoff,
                      dstride, dcoff, W, b_rs=4, b_mrs=5):
            dst_r.wr("dv")
            pbr[b_rs].rd("dv")
            pbr[b_mrs].rd("dv")
            src_r.rd("dv")
            for c in range(nch):
                s_ap = src[:, c * sstride + scoff: c * sstride + scoff + W]
                d_ap = dst[:, c * dstride + dcoff: c * dstride + dcoff + W]
                p.op("dv", (lambda e, s=s_ap, d=d_ap, W=W, b=b_rs:
                            e.tensor_tensor(d, s, pbk[b][:, 0:W], ALU.mult)))
                dv = p.op("dv", (lambda e, d=d_ap, W=W, b=b_mrs:
                                 e.tensor_tensor(d, d, pbk[b][:, 0:W],
                                                 ALU.subtract)), "dv")
            dst_r.note_wr("dv", dv)
            src_r.note_rd("dv", dv)
            pbr[b_rs].note_rd("dv", dv)
            pbr[b_mrs].note_rd("dv", dv)

        def shiftmix(u, u_r, xr, xr_r, stride, W, mu_col):
            xr_r.wr("dv")
            u_r.rd("dv")
            vecs_r.rd("dv")
            d3o = xr[:, :].rearrange("p (c t) -> p c t", c=8)[:, :, 1:W + 1]
            u3a = u[:, :].rearrange("p (c t) -> p c t", c=8)[:, :, 1:W + 1]
            u3b = u[:, :].rearrange("p (c t) -> p c t", c=8)[:, :, 0:W]
            p.op("dv", (lambda e, a=d3o, b=u3a, c=u3b: e.tensor_tensor(
                a, b, c, ALU.subtract)))
            for c in range(8):
                x_ap = xr[:, c * stride + 1: c * stride + 1 + W]
                u_ap = u[:, c * stride: c * stride + W]
                dv = p.op("dv", (lambda e, x=x_ap, u_=u_ap, c_=c:
                                 e.scalar_tensor_tensor(
                                     x, x, vecs[:, mu_col + c_:mu_col + c_ + 1],
                                     u_, ALU.mult, ALU.add)), "dv")
            xr_r.note_wr("dv", dv)
            u_r.note_rd("dv", dv)

        def mm_block(ring, offs, nm_groups, nk, rhs_hi, rhs_lo, rhs_r,
                     W, evac):
            n3 = 3 * nk
            for m in range(nm_groups):
                ring.load_pair_group(offs + m * nk * NBLK, nk)
                bk = m % 4
                s0 = ring.use_group()
                pbr[bk].wr("pe")
                notes = getattr(rhs_r, "chunk_notes", None)
                if notes is None or len(notes) < nk:
                    rhs_r.rd("pe")
                i3 = 0
                for k in range(nk):
                    if notes is not None and len(notes) >= nk:
                        p.wait("pe", "dv", notes[k])
                    whi, wlo = ring.blk(s0, 2 * k), ring.blk(s0, 2 * k + 1)
                    for wap, aap in ((whi, rhs_hi(k)), (whi, rhs_lo(k)),
                                     (wlo, rhs_hi(k))):
                        i3 += 1
                        pe = p.op("pe", (lambda e, wap=wap, aap=aap, bk=bk,
                                         W=W, st=(i3 == 1), sp=(i3 == n3):
                                         e.matmul(pbk[bk][:, 0:W], wap, aap,
                                                  start=st, stop=sp)),
                                  "pe" if i3 == n3 else None)
                ring.done_group(pe)
                pbr[bk].note_wr("pe", pe)
                if m == nm_groups - 1:
                    rhs_r.note_rd("pe", pe)
                evac(m, bk)

        def split_pair(src, src_r, hi, lo, pair_r, nch, sstride, scoff,
                      pstride, pcoff, W):
            pair_r.wr("dv")
            src_r.rd("dv")
            pair_r.chunk_notes = []
            for c in range(nch):
                s_ap = src[:, c * sstride + scoff: c * sstride + scoff + W]
                h_ap = hi[:, c * pstride + pcoff: c * pstride + pcoff + W]
                l_ap = lo[:, c * pstride + pcoff: c * pstride + pcoff + W]
                p.op("dv", (lambda e, s=s_ap, h=h_ap: e.tensor_copy(h, s)))
                dv = p.op("dv", (lambda e, s=s_ap, h=h_ap, l=l_ap:
                                 e.tensor_tensor(l, s, h, ALU.subtract)), "dv")
                pair_r.chunk_notes.append(dv)
            pair_r.note_wr("dv", dv)
            src_r.note_rd("dv", dv)

        with contextlib.ExitStack() as s1x:
            xi = sb("xi", [128, 4 * COLS], s1x)
            xi_r = Res(p)

            with contextlib.ExitStack() as wx:
                SPLC = 8 * NBLK
                NSPL = NBLOB // SPLC
                w_in = [wx.enter_context(nc.sbuf_tensor(
                    f"w_in{i}", [128, 1024], F32)) for i in range(2)]
                w_hi = [wx.enter_context(nc.sbuf_tensor(
                    f"w_hi{i}", [128, 1024], BF16)) for i in range(2)]
                w_lo = [wx.enter_context(nc.sbuf_tensor(
                    f"w_lo{i}", [128, 1024], BF16)) for i in range(2)]
                win_r = [Res(p), Res(p)]
                wp_r = [Res(p), Res(p)]
                for i_ in range(NSPL if "A" in PASSES or "C" in PASSES else 0):
                    off = i_ * SPLC
                    sl = i_ % 2
                    eng = "dv" if i_ % 2 == 0 else "gp"
                    esem = "dv" if eng == "dv" else "gp"
                    win_r[sl].wr("sp")
                    ldg = p.op("sp", (lambda e, off=off, sl=sl: e.dma_start(
                        bass.AP(w_in[sl], 0, [[1024, 128], [128, 8], [1, 128]]),
                        bass.AP(blob_h, off, [[128, 128], [NBLK, 8],
                                              [1, 128]]))), "ld", 16)
                    win_r[sl].note_wr("ld", ldg)
                    wp_r[sl].wr(eng)
                    win_r[sl].rd(eng)
                    p.op(eng, (lambda e, sl=sl: e.tensor_copy(
                        w_hi[sl][:, :], w_in[sl][:, :])))
                    cst = p.op(eng, (lambda e, sl=sl: e.tensor_tensor(
                        w_lo[sl][:, :], w_in[sl][:, :], w_hi[sl][:, :],
                        ALU.subtract)), esem)
                    wp_r[sl].note_wr(esem, cst)
                    win_r[sl].note_rd(esem, cst)
                    wp_r[sl].rd("gp")
                    p.op("gp", (lambda e, off=off, sl=sl: e.dma_start(
                        bass.AP(wsplit_h, 2 * off,
                                [[128, 128], [2 * NBLK, 8], [1, 128]]),
                        bass.AP(w_hi[sl], 0, [[1024, 128], [128, 8],
                                              [1, 128]]))), "st", 16)
                    stc = p.op("gp", (lambda e, off=off, sl=sl: e.dma_start(
                        bass.AP(wsplit_h, 2 * off + NBLK,
                                [[128, 128], [2 * NBLK, 8], [1, 128]]),
                        bass.AP(w_lo[sl], 0, [[1024, 128], [128, 8],
                                              [1, 128]]))), "st", 16)
                    wp_r[sl].note_rd("st", stc)
            p.fence()

            def split_ready(off_f32_end):
                return None

            with contextlib.ExitStack() as ax:
                pass

                ring = Ring(p, sb("ringA", [128, 32 * 128], ax, BF16),
                            wsplit_h, split_ready, nslots=32)
                pA_hi = sb("pA_hi", [128, 8 * AW], ax, BF16)
                pA_lo = sb("pA_lo", [128, 8 * AW], ax, BF16)
                pA_r = Res(p)
                pB_hi = sb("pB_hi", [128, 8 * AW], ax, BF16)
                pB_lo = sb("pB_lo", [128, 8 * AW], ax, BF16)
                pB_r = Res(p)
                tks = [sb(f"tkA{i}", [128, AW], ax) for i in range(2)]
                tk_rs = [Res(p), Res(p)]
                xb = [sb(f"xA{i}", [128, 8 * AW], ax) for i in range(2)]
                xb_r = [Res(p), Res(p)]
                ub = sb("uA", [128, 8 * AW], ax)
                ub_r = Res(p)
                xrb = sb("xrA", [128, 8 * AW], ax)
                xrb_r = Res(p)
                rb = sb("rA", [128, 8 * AW], ax)
                rb_r = Res(p)
                sqs = [sb(f"sqA{i}", [128, AW], ax) for i in range(2)]
                sq_rs = [Res(p), Res(p)]
                rA_ = sb("rowA", [1, AW], ax)
                rB_ = sb("rowB", [1, AW], ax)
                rC_ = sb("rowC", [1, AW], ax)
                rows_r = Res(p)
                rD_ = sb("rowD", [1, AW], ax)
                rE_ = sb("rowE", [1, AW], ax)
                rF_ = sb("rowF", [1, AW], ax)
                rows_r2 = Res(p)

                def ranges(j):
                    o0 = j * ATW
                    return dict(
                        o0=o0, oc0=o0 + (1 if j == 0 else 0),
                        u0=o0 - 1 if j > 0 else 0,
                        OW=ATW - (1 if j == 0 else 0))

                def head(j):
                    r = ranges(j)
                    u0, OW = r["u0"], r["OW"]
                    UW = OW + 1
                    par = j % 2
                    xt, xt_r = xb[par], xb_r[par]
                    xt_r.wr("sp")
                    ld = p.op("sp", (lambda e, u0=u0, UW=UW, xt=xt:
                                     e.dma_start(
                                         bass.AP(xt, 0, [[8 * AW, 128],
                                                 [AW, 8], [1, UW]]),
                                         bass.AP(x_h, u0, [[COLS, 128],
                                                 [128 * COLS, 8], [1, UW]]))),
                             "ld", 16)
                    xt_r.note_wr("ld", ld)
                    stats(xt, xt_r, 8, AW, 0, UW, inv1024[:, :], EPS5_COL,
                          True, sqs, sq_rs, rD_, rE_, rF_, rows_r2,
                          bks=(6, 7, 6, 7), defer_bcast=True)

                def prep(j):
                    r = ranges(j)
                    OW = r["OW"]
                    par = j % 2
                    xt, xt_r = xb[par], xb_r[par]
                    normalize(xt, xt_r, ub, ub_r, 8, AW, 0, AW, 0, OW + 1,
                              b_rs=6, b_mrs=7)
                    shiftmix(ub, ub_r, xrb, xrb_r, AW, OW, VECS["mu_r"][0])
                    split_pair(xrb, xrb_r, pA_hi, pA_lo, pA_r,
                               8, AW, 1, AW, 1, OW)

                def body(j):
                    r = ranges(j)
                    oc0, OW = r["oc0"], r["OW"]
                    par = j % 2
                    xt, xt_r = xb[par], xb_r[par]

                    rb_r.wr("ac")

                    def ev_sig(m, bk):
                        pbr[bk].rd("ac")
                        ac = p.op("ac", (lambda e, m=m, bk=bk, OW=OW:
                                         e.activation(
                                             rb[:, m * AW + 1:
                                                m * AW + 1 + OW],
                                             pbk[bk][:, 0:OW], AF.Sigmoid,
                                             bias=0.0, scale=1.0)), "ac")
                        pbr[bk].note_rd("ac", ac)
                        if m == 7:
                            rb_r.note_wr("ac", ac)
                    mm_block(ring, OFFS["Wr"], 8, 8,
                             lambda k: pA_hi[:, k * AW + 1: k * AW + 1 + OW],
                             lambda k: pA_lo[:, k * AW + 1: k * AW + 1 + OW],
                             pA_r, OW, ev_sig)

                    shiftmix(ub, ub_r, xrb, xrb_r, AW, OW, VECS["mu_k"][0])
                    split_pair(xrb, xrb_r, pB_hi, pB_lo, pB_r,
                               8, AW, 1, AW, 1, OW)

                    pA_r.wr("dv")
                    pA_r.chunk_notes = []

                    def ev_relu2(m, bk):
                        sl = m % 2
                        tk, tk_r = tks[sl], tk_rs[sl]
                        tk_r.wr("ac")
                        pbr[bk].rd("ac")
                        p.op("ac", (lambda e, bk=bk, tk=tk, OW=OW:
                                    e.activation(
                                        tk[:, 0:OW], pbk[bk][:, 0:OW],
                                        AF.Relu, bias=0.0, scale=1.0)), "ac")
                        ac = p.op("ac", (lambda e, tk=tk, OW=OW: e.activation(
                            tk[:, 0:OW], tk[:, 0:OW], AF.Square,
                            bias=0.0, scale=1.0)), "ac")
                        pbr[bk].note_rd("ac", ac)
                        tk_r.note_wr("ac", ac)
                        p.wait("dv", "ac", ac)
                        hp = pA_hi[:, m * AW + 1: m * AW + 1 + OW]
                        lp = pA_lo[:, m * AW + 1: m * AW + 1 + OW]
                        p.op("dv", (lambda e, hp=hp, tk=tk, OW=OW:
                                    e.tensor_copy(hp, tk[:, 0:OW])))
                        dv = p.op("dv", (lambda e, hp=hp, lp=lp, tk=tk,
                                         OW=OW: e.tensor_tensor(
                                             lp, tk[:, 0:OW], hp,
                                             ALU.subtract)), "dv")
                        tk_r.note_rd("dv", dv)
                        pA_r.chunk_notes.append(dv)
                        if m == 7:
                            pA_r.note_wr("dv", dv)
                    mm_block(ring, OFFS["Wk"], 8, 8,
                             lambda k: pB_hi[:, k * AW + 1: k * AW + 1 + OW],
                             lambda k: pB_lo[:, k * AW + 1: k * AW + 1 + OW],
                             pB_r, OW, ev_relu2)

                    if j + 1 < NA:
                        head(j + 1)

                    def ev_y1(m, bk):
                        pbr[bk].rd("dv")
                        rb_r.rd("dv")
                        y1_r.wr("dv")
                        xt_r.rd("dv")
                        dst = y1[:, m * COLS + oc0: m * COLS + oc0 + OW]
                        p.op("dv", (lambda e, m=m, bk=bk, dst=dst, OW=OW:
                                    e.scalar_tensor_tensor(
                                        dst, pbk[bk][:, 0:OW], 0.5,
                                        rb[:, m * AW + 1: m * AW + 1 + OW],
                                        ALU.mult, ALU.mult)))
                        dv = p.op("dv", (lambda e, dst=dst, OW=OW, xt=xt,
                                         m=m: e.tensor_tensor(
                                             dst, dst,
                                             xt[:, m * AW + 1:
                                                m * AW + 1 + OW],
                                             ALU.add)), "dv")
                        pbr[bk].note_rd("dv", dv)
                        y1_r.note_wr("dv", dv)
                        if m == 7:
                            rb_r.note_rd("dv", dv)
                            xt_r.note_rd("dv", dv)
                    mm_block(ring, OFFS["Wv"], 8, 8,
                             lambda k: pA_hi[:, k * AW + 1: k * AW + 1 + OW],
                             lambda k: pA_lo[:, k * AW + 1: k * AW + 1 + OW],
                             pA_r, OW, ev_y1)

                    if j + 1 < NA:
                        sr2 = rows_r2
                        pbr[6].wr("pe")
                        sr2.rd("pe")
                        pe = p.op("pe", (lambda e, W=ranges(j + 1)["OW"] + 1:
                                         e.matmul(pbk[6][:, 0:W],
                                                  ones1[0:1, :],
                                                  rE_[0:1, 0:W],
                                                  start=True, stop=True)),
                                  "pe")
                        pbr[6].note_wr("pe", pe)
                        sr2.note_rd("pe", pe)
                        pbr[7].wr("pe")
                        pe = p.op("pe", (lambda e, W=ranges(j + 1)["OW"] + 1:
                                         e.matmul(pbk[7][:, 0:W],
                                                  ones1[0:1, :],
                                                  rF_[0:1, 0:W],
                                                  start=True, stop=True)),
                                  "pe")
                        pbr[7].note_wr("pe", pe)
                        sr2.note_rd("pe", pe)
                    stats(y1, y1_r, 8, COLS, oc0, OW, inv1024[:, :],
                          EPS5_COL, True, sqs, sq_rs, rA_, rB_, rC_, rows_r)
                    rows_r.rd("sp")
                    st = p.op("sp", (lambda e, oc0=oc0, OW=OW: e.dma_start(
                        vstat_h[0:1, oc0:oc0 + OW], rB_[0:1, 0:OW])),
                        "ld", 16)
                    st = p.op("sp", (lambda e, oc0=oc0, OW=OW: e.dma_start(
                        vstat_h[1:2, oc0:oc0 + OW], rC_[0:1, 0:OW])),
                        "ld", 16)
                    rows_r.note_rd("ld", st)

                    vb, vb_r = xt, xt_r
                    normalize(y1, y1_r, vb, vb_r, 8, COLS, oc0, AW, 0, OW)
                    split_pair(vb, vb_r, pB_hi, pB_lo, pB_r,
                               8, AW, 0, AW, 0, OW)

                    if j + 1 < NA:
                        prep(j + 1)

                    rb_r.wr("dv")

                    def ev_down(m, bk):
                        pbr[bk].rd("ac")
                        ac = p.op("ac", (lambda e, m=m, bk=bk, OW=OW:
                                         e.activation(
                                             rb[:, m * AW: m * AW + OW],
                                             pbk[bk][:, 0:OW], AF.Identity,
                                             bias=vecs[:, VECS["down_b"][0] + m:
                                                       VECS["down_b"][0] + m + 1],
                                             scale=1.0)), "ac")
                        pbr[bk].note_rd("ac", ac)
                        if m == 3:
                            rb_r.note_wr("ac", ac)
                    mm_block(ring, OFFS["down"], 4, 8,
                             lambda k: pB_hi[:, k * AW: k * AW + OW],
                             lambda k: pB_lo[:, k * AW: k * AW + OW],
                             pB_r, OW, ev_down)
                    stats(rb, rb_r, 4, AW, 0, OW, inv512[:, :], EPS6_COL,
                          False, sqs, sq_rs, rA_, rB_, rC_, rows_r)
                    xi_r.wr("dv")
                    pbr[4].rd("dv")
                    for m in range(4):
                        dv = p.op("dv", (lambda e, m=m, OW=OW, oc0=oc0:
                                         e.scalar_tensor_tensor(
                                             xi[:, m * COLS + oc0:
                                                m * COLS + oc0 + OW],
                                             rb[:, m * AW: m * AW + OW],
                                             vecs[:, VECS["norm1_w"][0] + m:
                                                  VECS["norm1_w"][0] + m + 1],
                                             pbk[4][:, 0:OW],
                                             ALU.mult, ALU.mult)), "dv")
                    xi_r.note_wr("dv", dv)
                    rb_r.note_rd("dv", dv)
                    pbr[4].note_rd("dv", dv)

                if "A" in PASSES:
                    head(0)
                    pbr[6].wr("pe")
                    rows_r2.rd("pe")
                    pe = p.op("pe", (lambda e, W=ranges(0)["OW"] + 1:
                                     e.matmul(pbk[6][:, 0:W], ones1[0:1, :],
                                              rE_[0:1, 0:W],
                                              start=True, stop=True)), "pe")
                    pbr[6].note_wr("pe", pe)
                    rows_r2.note_rd("pe", pe)
                    pbr[7].wr("pe")
                    pe = p.op("pe", (lambda e, W=ranges(0)["OW"] + 1:
                                     e.matmul(pbk[7][:, 0:W], ones1[0:1, :],
                                              rF_[0:1, 0:W],
                                              start=True, stop=True)), "pe")
                    pbr[7].note_wr("pe", pe)
                    rows_r2.note_rd("pe", pe)
                    prep(0)
                    for j in range(NA):
                        body(j)
                xi_r.wr("dv")
                dv = p.op("dv", lambda e: e.memset(
                    bass.AP(xi, NCOMP, [[4 * COLS, 128], [COLS, 4],
                                        [1, COLS - NCOMP]]), 0.0), "dv")
                xi_r.note_wr("dv", dv)

                for t in ("y1", "xi"):
                    if t in taps:
                        src, res, nch = ((y1, y1_r, 8) if t == "y1"
                                         else (xi, xi_r, 4))
                        res.rd("gp")
                        st = p.op("gp", (lambda e, src=src, nch=nch, t=t:
                                         e.dma_start(
                                             bass.AP(taps[t], 0,
                                                     [[COLS, 128],
                                                      [128 * COLS, nch],
                                                      [1, COLS]]),
                                             bass.AP(src, 0,
                                                     [[nch * COLS, 128],
                                                      [COLS, nch],
                                                      [1, COLS]]))), "st", 16)
                        res.note_rd("st", st)
            p.fence()

            with contextlib.ExitStack() as bx:
                rows = sb("rows_s", [1, 512], bx)
                ident = sb("ident_s", [128, 128], bx)
                p.op("sp", dma(rows[:, :], rows_h[:, :]), "ld", 16)
                p.op("sp", dma(ident[:, :], ident_h[:, :]), "ld", 16)
                wqk = [sb(f"wqk{b_}", [128, 1024], bx) for b_ in (0, 1)]
                wv = [sb(f"wv{b_}", [128, 512], bx) for b_ in (0, 1)]
                wpj = [sb(f"wpj{b_}", [128, 512], bx) for b_ in (0, 1)]
                tabs = {(bi, cls): sb(f"tab{bi}{cls}", [128, 1024], bx)
                        for bi in (0, 1) for cls in (0, 1, 2)}
                npre_b = 8 * NBLK + 2 * 128 * 256 + 4 * NBLK
                for bi in ((0, 1) if "B" in PASSES else ()):
                    off = bi * npre_b
                    p.op("sp", dma(wqk[bi][:, :], bass.AP(
                        pre_h, off, [[128, 128], [NBLK, 8], [1, 128]])),
                        "ld", 16)
                    p.op("sp", dma(wv[bi][:, :], bass.AP(
                        pre_h, off + 8 * NBLK,
                        [[256, 128], [128 * 256, 2], [1, 256]])), "ld", 16)
                    p.op("sp", dma(wpj[bi][:, :], bass.AP(
                        pre_h, off + 8 * NBLK + 2 * 128 * 256,
                        [[128, 128], [NBLK, 4], [1, 128]])), "ld", 16)
                toff = 0
                ldB = None
                for bi in ((0, 1) if "B" in PASSES else ()):
                    for cls in (0, 1, 2):
                        C = CLS_W[bi][cls]
                        ldB = p.op("sp", dma(
                            bass.AP(tabs[(bi, cls)], 0,
                                    [[1024, C], [128, 8], [1, C]]),
                            bass.AP(atab_h, toff,
                                    [[C, C], [C * C, 8], [1, C]])), "ld", 16)
                        toff += 8 * C * C
                p.wait("pe", "ld", ldB)
                p.wait("dv", "ld", ldB)

                qk_sb = [sb(f"qk{i}", [128, 512], bx) for i in range(2)]
                vT_sb = [sb(f"vT{i}", [128, 256], bx) for i in range(4)]
                S_sb = [sb(f"S{i}", [128, 1024], bx) for i in range(2)]
                P_sb = [sb(f"P{i}", [128, 1024], bx) for i in range(2)]
                PT_sb = [sb(f"PT{i}", [128, 1024], bx) for i in range(2)]
                O_sb = [sb(f"O{i}", [128, 256], bx) for i in range(2)]
                nmx = [sb(f"nmx{i}", [128, 8], bx) for i in range(2)]
                sms = [sb(f"sms{i}", [128, 8], bx) for i in range(2)]
                rcp = [sb(f"rcp{i}", [128, 8], bx) for i in range(2)]
                RR = lambda: [Res(p), Res(p)]
                qk_r, S_r, P_r = RR(), RR(), RR()
                vT_r = [Res(p) for _ in range(4)]
                PT_r, O_r, nmx_r, sms_r, rcp_r = RR(), RR(), RR(), RR(), RR()

                ci_all = [(bi, s, C, cls) for bi in (0, 1)
                          for (s, C, cls) in CHUNKS[bi]] if "B" in PASSES else []

                def b_qkvT(idx):
                    bi, s, C, cls = ci_all[idx]
                    par = idx % 2
                    pv3 = idx % 4
                    bqk = VECS[f"qkv{bi+1}_b_qk"][0]
                    pbr[0].wr("pe")
                    xi_r.rd("pe")
                    for m in range(4):
                        for k in range(2):
                            pe = p.op("pe", (lambda e, m=m, k=k, bi=bi, s=s,
                                             C=C: e.matmul(
                                                 pbk[0][:, m * 128:m * 128 + C],
                                                 wqk[bi][:, (m * 2 + k) * 128:
                                                         (m * 2 + k + 1) * 128],
                                                 xi[:, (bi * 2 + k) * COLS + s:
                                                    (bi * 2 + k) * COLS + s + C],
                                                 start=(k == 0), stop=(k == 1))),
                                      "pe" if k == 1 else None)
                    pbr[0].note_wr("pe", pe)
                    qk_r[par].wr("dv")
                    pbr[0].rd("dv")
                    for m in range(4):
                        dv = p.op("dv", (lambda e, m=m, C=C, par=par,
                                         bqk=bqk: e.tensor_scalar_add(
                                             qk_sb[par][:, m * 128:m * 128 + C],
                                             pbk[0][:, m * 128:m * 128 + C],
                                             vecs[:, bqk + m:bqk + m + 1])),
                                  "dv")
                    pbr[0].note_rd("dv", dv)
                    qk_r[par].note_wr("dv", dv)
                    pbr[1].wr("pe")
                    for k in range(2):
                        p.op("pe", (lambda e, k=k, bi=bi, s=s, C=C: e.matmul(
                            pbk[1][0:C, 0:256],
                            xi[:, (bi * 2 + k) * COLS + s:
                               (bi * 2 + k) * COLS + s + C],
                            wv[bi][:, k * 256:(k + 1) * 256],
                            start=(k == 0), stop=False)))
                    pe = p.op("pe", (lambda e, bi=bi, C=C: e.matmul(
                        pbk[1][0:C, 0:256], ones1[0:1, 0:C],
                        rows[0:1, bi * 256:(bi + 1) * 256],
                        start=False, stop=True)), "pe")
                    pbr[1].note_wr("pe", pe)
                    xi_r.note_rd("pe", pe)
                    vT_r[pv3].wr("dv")
                    pbr[1].rd("dv")
                    dv = p.op("dv", (lambda e, C=C, pv3=pv3: e.tensor_copy(
                        vT_sb[pv3][0:C, 0:256], pbk[1][0:C, 0:256])), "dv")
                    pbr[1].note_rd("dv", dv)
                    vT_r[pv3].note_wr("dv", dv)

                def b_scores(idx):
                    bi, s, C, cls = ci_all[idx]
                    par = idx % 2
                    for b_ in (2, 3, 4, 5):
                        pbr[b_].wr("pe")
                    qk_r[par].rd("pe")
                    for h in range(8):
                        bk = 2 + h % 4
                        rg = 32 * (h % 4)
                        sc = (h // 4) * 128
                        pe = p.op("pe", (lambda e, h=h, bk=bk, rg=rg, sc=sc,
                                         C=C, par=par: e.matmul(
                                             pbk[bk][0:C, sc:sc + C],
                                             qk_sb[par][rg:rg + 32,
                                                        (h // 4) * 128:
                                                        (h // 4) * 128 + C],
                                             qk_sb[par][rg:rg + 32,
                                                        (2 + h // 4) * 128:
                                                        (2 + h // 4) * 128 + C],
                                             start=True, stop=True,
                                             tile_position=(rg, 0))),
                                  "pe" if h == 7 else None)
                        if h == 7:
                            for b_ in (2, 3, 4, 5):
                                pbr[b_].note_wr("pe", pe)
                            qk_r[par].note_rd("pe", pe)
                    S_r[par].wr("dv")
                    tab = tabs[(bi, cls)]
                    for h in range(8):
                        bk = 2 + h % 4
                        sc = (h // 4) * 128
                        pbr[bk].rd("dv")
                        dv = p.op("dv", (lambda e, h=h, bk=bk, sc=sc, C=C,
                                         par=par, tab=tab:
                                         e.scalar_tensor_tensor(
                                             S_sb[par][0:C, h * 128:h * 128 + C],
                                             pbk[bk][0:C, sc:sc + C],
                                             SCALE,
                                             tab[0:C, h * 128:h * 128 + C],
                                             ALU.mult, ALU.add)), "dv")
                        pbr[bk].note_rd("dv", dv)
                    S_r[par].note_wr("dv", dv)
                    nmx_r[par].wr("dv")
                    dv = p.op("dv", (lambda e, C=C, par=par: e.tensor_reduce(
                        nmx[par][0:C, 0:8],
                        bass.AP(S_sb[par], 0, [[1024, C], [128, 8], [1, C]]),
                        AX.X, ALU.max, negate=True)), "dv")
                    nmx_r[par].note_wr("dv", dv)
                    P_r[par].wr("ac")
                    sms_r[par].wr("ac")
                    S_r[par].rd("ac")
                    nmx_r[par].rd("ac")
                    for h in range(8):
                        ac = p.op("ac", (lambda e, h=h, C=C, par=par:
                                         e.activation(
                                             P_sb[par][0:C, h * 128:h * 128 + C],
                                             S_sb[par][0:C, h * 128:h * 128 + C],
                                             AF.Exp,
                                             bias=nmx[par][0:C, h:h + 1],
                                             scale=1.0,
                                             accum_out=sms[par][0:C, h:h + 1])),
                                  "ac")
                    P_r[par].note_wr("ac", ac)
                    sms_r[par].note_wr("ac", ac)
                    S_r[par].note_rd("ac", ac)
                    nmx_r[par].note_rd("ac", ac)

                def b_back(idx):
                    bi, s, C, cls = ci_all[idx]
                    par = idx % 2
                    pv3 = idx % 4
                    pjb = VECS[f"proj{bi+1}_b"][0]
                    rcp_r[par].wr("dv")
                    sms_r[par].rd("dv")
                    dv = p.op("dv", (lambda e, C=C, par=par: e.reciprocal(
                        rcp[par][0:C, 0:8], sms[par][0:C, 0:8])), "dv")
                    rcp_r[par].note_wr("dv", dv)
                    sms_r[par].note_rd("dv", dv)
                    p.flush("dv")
                    P_r[par].rd("dv")
                    for h in range(8):
                        dv = p.op("dv", (lambda e, h=h, C=C, par=par:
                                         e.tensor_scalar_mul(
                                             P_sb[par][0:C, h * 128:h * 128 + C],
                                             P_sb[par][0:C, h * 128:h * 128 + C],
                                             rcp[par][0:C, h:h + 1])), "dv")
                    P_r[par].note_wr("dv", dv)
                    rcp_r[par].note_rd("dv", dv)
                    pbr[6].wr("pe")
                    pbr[7].wr("pe")
                    P_r[par].rd("pe")
                    for h in range(8):
                        bk = 6 + h // 4
                        pe = p.op("pe", (lambda e, h=h, bk=bk, C=C, par=par:
                                         e.transpose(
                                             pbk[bk][0:C, (h % 4) * 128:
                                                     (h % 4) * 128 + C],
                                             P_sb[par][0:C, h * 128:h * 128 + C],
                                             ident[0:C, 0:C])),
                                  "pe" if h in (3, 7) else None)
                        if h in (3, 7):
                            pbr[bk].note_wr("pe", pe)
                        if h == 7:
                            P_r[par].note_rd("pe", pe)
                    PT_r[par].wr("dv")
                    for g in (0, 1):
                        bk = 6 + g
                        pbr[bk].rd("dv")
                        dv = p.op("dv", (lambda e, g=g, bk=bk, C=C, par=par:
                                         e.tensor_copy(
                                             bass.AP(PT_sb[par], g * 512,
                                                     [[1024, C], [128, 4],
                                                      [1, C]]),
                                             bass.AP(pbk[bk], 0,
                                                     [[512, C], [128, 4],
                                                      [1, C]]))), "dv")
                        pbr[bk].note_rd("dv", dv)
                    PT_r[par].note_wr("dv", dv)
                    pbr[6].wr("pe")
                    vT_r[pv3].rd("pe")
                    PT_r[par].rd("pe")
                    for h in range(8):
                        cg = 32 * (h % 4)
                        pe = p.op("pe", (lambda e, h=h, cg=cg, C=C, par=par,
                                         pv3=pv3: e.matmul(
                                             pbk[6][cg:cg + 32,
                                                    (h // 4) * 128:
                                                    (h // 4) * 128 + C],
                                             vT_sb[pv3][0:C, h * 32:h * 32 + 32],
                                             PT_sb[par][0:C,
                                                        h * 128:h * 128 + C],
                                             start=True, stop=True,
                                             tile_position=(0, cg))),
                                  "pe" if h == 7 else None)
                    pbr[6].note_wr("pe", pe)
                    vT_r[pv3].note_rd("pe", pe)
                    PT_r[par].note_rd("pe", pe)
                    O_r[par].wr("dv")
                    pbr[6].rd("dv")
                    for g in (0, 1):
                        dv = p.op("dv", (lambda e, g=g, C=C, par=par:
                                         e.tensor_copy(
                                             O_sb[par][:, g * 128:g * 128 + C],
                                             pbk[6][:, g * 128:g * 128 + C])),
                                  "dv")
                    pbr[6].note_rd("dv", dv)
                    O_r[par].note_wr("dv", dv)
                    pbr[7].wr("pe")
                    O_r[par].rd("pe")
                    for m in range(2):
                        for k in range(2):
                            pe = p.op("pe", (lambda e, m=m, k=k, bi=bi, C=C,
                                             par=par: e.matmul(
                                                 pbk[7][:, m * 128:m * 128 + C],
                                                 wpj[bi][:, (m * 2 + k) * 128:
                                                         (m * 2 + k + 1) * 128],
                                                 O_sb[par][:, k * 128:
                                                           k * 128 + C],
                                                 start=(k == 0), stop=(k == 1))),
                                      "pe" if k == 1 else None)
                    pbr[7].note_wr("pe", pe)
                    O_r[par].note_rd("pe", pe)
                    o_r.wr("dv")
                    pbr[7].rd("dv")
                    for m in range(2):
                        dv = p.op("dv", (lambda e, m=m, bi=bi, s=s, C=C,
                                         pjb=pjb: e.tensor_scalar_add(
                                             o_full[:, (bi * 2 + m) * COLS + s:
                                                    (bi * 2 + m) * COLS + s + C],
                                             pbk[7][:, m * 128:m * 128 + C],
                                             vecs[:, pjb + m:pjb + m + 1])),
                                  "dv")
                    pbr[7].note_rd("dv", dv)
                    o_r.note_wr("dv", dv)
                    if idx == BDUMP:
                        for nm_, buf, res in (("qk", qk_sb[par], qk_r[par]),
                                              ("vT", vT_sb[par], vT_r[par]),
                                              ("S", S_sb[par], S_r[par]),
                                              ("P", P_sb[par], P_r[par]),
                                              ("O", O_sb[par], O_r[par]),
                                              ("nmx", nmx[par], nmx_r[par]),
                                              ("sms", sms[par], sms_r[par])):
                            res.rd("gp")
                            st = p.op("gp", (lambda e, buf=buf, nm_=nm_:
                                             e.dma_start(
                                                 bdump_h[nm_][:, :],
                                                 buf[:, :])), "st", 16)
                            res.note_rd("st", st)

                if ci_all:
                    b_qkvT(0)
                    if len(ci_all) > 1:
                        b_qkvT(1)
                    b_scores(0)
                for i in range(len(ci_all)):
                    if i + 2 < len(ci_all):
                        b_qkvT(i + 2)
                    if i + 1 < len(ci_all):
                        b_scores(i + 1)
                    b_back(i)
                if "o" in taps:
                    o_r.rd("gp")
                    st = p.op("gp", lambda e: e.dma_start(
                        bass.AP(taps["o"], 0, [[COLS, 128], [128 * COLS, 4],
                                               [1, COLS]]),
                        bass.AP(o_full, 0, [[4 * COLS, 128], [COLS, 4],
                                            [1, COLS]])), "st", 16)
                    o_r.note_rd("st", st)
            p.fence()
        p.fence()

        with contextlib.ExitStack() as cx:
            ring = Ring(p, sb("ringC", [128, 48 * 128], cx, BF16),
                        wsplit_h, split_ready)
            s1 = sb("s1C", [128, 8 * CW], cx)
            s2 = sb("s2C", [128, 8 * CW], cx)
            s2z = sb("s2zC", [128, 8 * CW], cx)
            s3 = sb("s3C", [128, 8 * CW], cx)
            s5 = sb("s5C", [128, 8 * CW], cx)
            pairC_hi = sb("pairC_hi", [128, 8 * CW], cx, BF16)
            pairC_lo = sb("pairC_lo", [128, 8 * CW], cx, BF16)
            pairC_r = Res(p)
            hidC_hi = sb("hidC_hi", [128, 16 * CW], cx, BF16)
            hidC_lo = sb("hidC_lo", [128, 16 * CW], cx, BF16)
            hid_r = Res(p)
            sqCs = [sb(f"sqC{i}", [128, CW], cx) for i in range(2)]
            tC, tC_r = sqCs, None
            rAc = sb("rAC", [1, CW], cx)
            rBc = sb("rBC", [1, CW], cx)
            rCc = sb("rCC", [1, CW], cx)
            s1_r, s2_r, s2z_r, s3_r, s5_r = (Res(p) for _ in range(5))
            sq_r2s = [Res(p), Res(p)]
            tC_r = sq_r2s
            rows_r2 = Res(p)

            def cst(j):
                z0 = 19 if j == 0 else CTW * j - 1
                ZW = min(NCOMP, CTW * (j + 1)) - z0
                lc0 = 1 if j == 0 else 0
                return z0, ZW, lc0, ZW - lc0, ZW - 1

            def c_head(j):
                z0, ZW, lc0, CWj, OWj = cst(j)
                rows_r2.wr("sp")
                ldv = p.op("sp", (lambda e, z0=z0, lc0=lc0, CWj=CWj:
                                  e.dma_start(rBc[0:1, 0:CWj],
                                              vstat_h[0:1, z0 + lc0:
                                                      z0 + lc0 + CWj])),
                           "ld", 16)
                ldv = p.op("sp", (lambda e, z0=z0, lc0=lc0, CWj=CWj:
                                  e.dma_start(rCc[0:1, 0:CWj],
                                              vstat_h[1:2, z0 + lc0:
                                                      z0 + lc0 + CWj])),
                           "ld", 16)
                rows_r2.note_wr("ld", ldv)
                pbr[4].wr("pe")
                pbr[5].wr("pe")
                rows_r2.rd("pe")
                pe = p.op("pe", (lambda e, CWj=CWj: e.matmul(
                    pbk[4][:, 0:CWj], ones1[0:1, :], rBc[0:1, 0:CWj],
                    start=True, stop=True)))
                pe = p.op("pe", (lambda e, CWj=CWj: e.matmul(
                    pbk[5][:, 0:CWj], ones1[0:1, :], rCc[0:1, 0:CWj],
                    start=True, stop=True)), "pe")
                pbr[4].note_wr("pe", pe)
                pbr[5].note_wr("pe", pe)
                rows_r2.note_rd("pe", pe)
                normalize(y1, y1_r, s1, s1_r, 8, COLS, z0 + lc0, CW, lc0, CWj)
                split_pair(o_full, o_r, pairC_hi, pairC_lo, pairC_r,
                           4, COLS, z0 + lc0, CW, lc0, CWj)

            def c_up(j):
                z0, ZW, lc0, CWj, OWj = cst(j)
                s2_r.wr("dv")

                def ev_w1s(m, bk):
                    pbr[bk].rd("dv")
                    s1_r.rd("dv")
                    dv = p.op("dv", (lambda e, m=m, bk=bk, lc0=lc0, CWj=CWj:
                                     e.scalar_tensor_tensor(
                                         s2[:, m * CW + lc0: m * CW + lc0 + CWj],
                                         pbk[bk][:, 0:CWj],
                                         vecs[:, VECS["up_b"][0] + m:
                                              VECS["up_b"][0] + m + 1],
                                         s1[:, m * CW + lc0:
                                            m * CW + lc0 + CWj],
                                         ALU.add, ALU.add)), "dv")
                    pbr[bk].note_rd("dv", dv)
                    if m == 7:
                        s2_r.note_wr("dv", dv)
                        s1_r.note_rd("dv", dv)
                mm_block(ring, OFFS["up"], 8, 4,
                         lambda k: pairC_hi[:, k * CW + lc0: k * CW + lc0 + CWj],
                         lambda k: pairC_lo[:, k * CW + lc0: k * CW + lc0 + CWj],
                         pairC_r, CWj, ev_w1s)

            def c_stats_w1s(j):
                z0, ZW, lc0, CWj, OWj = cst(j)
                return stats(s2, s2_r, 8, CW, lc0, CWj, inv1024[:, :],
                             EPS6_COL, False, sqCs, sq_r2s, rAc, rBc, rCc,
                             rows_r2, bks=(6, 7, 4, 5), defer_bcast=True)

            def c_norm2(j):
                z0, ZW, lc0, CWj, OWj = cst(j)
                s3_r.wr("dv")
                pbr[4].rd("dv")
                for m in range(8):
                    dv = p.op("dv", (lambda e, m=m, lc0=lc0, CWj=CWj:
                                     e.scalar_tensor_tensor(
                                         s3[:, m * CW + lc0: m * CW + lc0 + CWj],
                                         s2[:, m * CW + lc0: m * CW + lc0 + CWj],
                                         vecs[:, VECS["norm2_w"][0] + m:
                                              VECS["norm2_w"][0] + m + 1],
                                         pbk[4][:, 0:CWj], ALU.mult,
                                         ALU.mult)), "dv")
                s3_r.note_wr("dv", dv)
                pbr[4].note_rd("dv", dv)
                split_pair(s3, s3_r, pairC_hi, pairC_lo, pairC_r,
                           8, CW, lc0, CW, lc0, CWj)

            def c_ffn1(j):
                z0, ZW, lc0, CWj, OWj = cst(j)

                def ev_gelu(m, bk):
                    sl = m % 2
                    tC_r[sl].wr("ac")
                    pbr[bk].rd("ac")
                    ac = p.op("ac", (lambda e, m=m, bk=bk, sl=sl, lc0=lc0,
                                     CWj=CWj: e.activation(
                                         tC[sl][:, 0:CWj],
                                         pbk[bk][:, 0:CWj], AF.Gelu,
                                         bias=vecs[:, VECS["ffn_b1"][0] + m:
                                                   VECS["ffn_b1"][0] + m + 1],
                                         scale=1.0)), "ac")
                    pbr[bk].note_rd("ac", ac)
                    tC_r[sl].note_wr("ac", ac)
                    if m == 0:
                        hid_r.wr("dv")
                    tC_r[sl].rd("dv")
                    hp = hidC_hi[:, m * CW + lc0: m * CW + lc0 + CWj]
                    lp = hidC_lo[:, m * CW + lc0: m * CW + lc0 + CWj]
                    p.op("dv", (lambda e, sl=sl, hp=hp, CWj=CWj:
                                e.tensor_copy(hp, tC[sl][:, 0:CWj])))
                    dv = p.op("dv", (lambda e, sl=sl, hp=hp, lp=lp, CWj=CWj:
                                     e.tensor_tensor(lp, tC[sl][:, 0:CWj], hp,
                                                     ALU.subtract)), "dv")
                    tC_r[sl].note_rd("dv", dv)
                    if m == 15:
                        hid_r.note_wr("dv", dv)
                mm_block(ring, OFFS["ffn1"], 16, 8,
                         lambda k: pairC_hi[:, k * CW + lc0: k * CW + lc0 + CWj],
                         lambda k: pairC_lo[:, k * CW + lc0: k * CW + lc0 + CWj],
                         pairC_r, CWj, ev_gelu)

            def c_ffn2(j):
                z0, ZW, lc0, CWj, OWj = cst(j)
                s5_r.wr("dv")
                for m in range(8):
                    bk = m % 4
                    pbr[bk].wr("pe")
                    hid_r.rd("pe")
                    for half in range(2):
                        ring.load_pair_group(
                            OFFS["ffn2"] + (m * 16 + half * 8) * NBLK, 8)
                        s0g = ring.use_group()
                        for kk in range(8):
                            k = half * 8 + kk
                            whi = ring.blk(s0g, 2 * kk)
                            wlo = ring.blk(s0g, 2 * kk + 1)
                            ahi = hidC_hi[:, k * CW + lc0: k * CW + lc0 + CWj]
                            alo = hidC_lo[:, k * CW + lc0: k * CW + lc0 + CWj]
                            for i3, (wap, aap) in enumerate(
                                    ((whi, ahi), (whi, alo), (wlo, ahi))):
                                pe = p.op("pe", (lambda e, wap=wap, aap=aap,
                                                 bk=bk, CWj=CWj,
                                                 st=(k == 0 and i3 == 0),
                                                 sp=(k == 15 and i3 == 2):
                                                 e.matmul(pbk[bk][:, 0:CWj],
                                                          wap, aap,
                                                          start=st, stop=sp)),
                                          "pe" if (kk == 7 and i3 == 2)
                                          else None)
                        ring.done_group(pe)
                    pbr[bk].note_wr("pe", pe)
                    if m == 7:
                        hid_r.note_rd("pe", pe)
                    pbr[bk].rd("dv")
                    s2_r.rd("dv")
                    y1_r.rd("dv")
                    p.op("dv", (lambda e, m=m, bk=bk, lc0=lc0, CWj=CWj:
                                e.scalar_tensor_tensor(
                                    s5[:, m * CW + lc0: m * CW + lc0 + CWj],
                                    pbk[bk][:, 0:CWj],
                                    vecs[:, VECS["ffn_b2"][0] + m:
                                         VECS["ffn_b2"][0] + m + 1],
                                    s2[:, m * CW + lc0: m * CW + lc0 + CWj],
                                    ALU.add, ALU.add)))
                    dv = p.op("dv", (lambda e, m=m, z0=z0, lc0=lc0, CWj=CWj:
                                     e.tensor_tensor(
                                         s5[:, m * CW + lc0: m * CW + lc0 + CWj],
                                         s5[:, m * CW + lc0: m * CW + lc0 + CWj],
                                         y1[:, m * COLS + z0 + lc0:
                                            m * COLS + z0 + lc0 + CWj],
                                         ALU.add)), "dv")
                    pbr[bk].note_rd("dv", dv)
                    if m == 7:
                        s2_r.note_rd("dv", dv)
                        y1_r.note_rd("dv", dv)
                s5_r.note_wr("dv", dv)
                if j == 0:
                    dv = p.op("dv", lambda e: e.memset(
                        bass.AP(s5, 0, [[8 * CW, 128], [CW, 8], [1, 1]]),
                        0.0), "dv")
                    s5_r.note_wr("dv", dv)

            def c_stats_ln3(j):
                z0, ZW, lc0, CWj, OWj = cst(j)
                return stats(s5, s5_r, 8, CW, 0, ZW, inv1024[:, :], EPS5_COL,
                             True, sqCs, sq_r2s, rAc, rBc, rCc, rows_r2,
                             bks=(6, 7, 4, 5), defer_bcast=True)

            def c_norm3r(j):
                z0, ZW, lc0, CWj, OWj = cst(j)
                normalize(s5, s5_r, s2z, s2z_r, 8, CW, 0, CW, 0, ZW)
                shiftmix(s2z, s2z_r, s3, s3_r, CW, OWj, VECS["mu_r"][0])
                split_pair(s3, s3_r, pairC_hi, pairC_lo, pairC_r,
                           8, CW, 1, CW, 1, OWj)

            def c_wr(j):
                z0, ZW, lc0, CWj, OWj = cst(j)
                s1_r.wr("ac")

                def ev_sig2(m, bk):
                    pbr[bk].rd("ac")
                    ac = p.op("ac", (lambda e, m=m, bk=bk, OWj=OWj:
                                     e.activation(
                                         s1[:, m * CW + 1: m * CW + 1 + OWj],
                                         pbk[bk][:, 0:OWj], AF.Sigmoid,
                                         bias=0.0, scale=1.0)), "ac")
                    pbr[bk].note_rd("ac", ac)
                    if m == 7:
                        s1_r.note_wr("ac", ac)
                mm_block(ring, OFFS["Wr"], 8, 8,
                         lambda k: pairC_hi[:, k * CW + 1: k * CW + 1 + OWj],
                         lambda k: pairC_lo[:, k * CW + 1: k * CW + 1 + OWj],
                         pairC_r, OWj, ev_sig2)

            def c_shiftk(j):
                z0, ZW, lc0, CWj, OWj = cst(j)
                shiftmix(s2z, s2z_r, s3, s3_r, CW, OWj, VECS["mu_k"][0])
                split_pair(s3, s3_r, pairC_hi, pairC_lo, pairC_r,
                           8, CW, 1, CW, 1, OWj)

            def c_wk(j):
                z0, ZW, lc0, CWj, OWj = cst(j)
                hid_r.wr("dv")
                hid_r.wr("ac")

                def ev_relu2b(m, bk):
                    sl = m % 2
                    tC_r[sl].wr("ac")
                    pbr[bk].rd("ac")
                    p.op("ac", (lambda e, bk=bk, sl=sl, OWj=OWj:
                                e.activation(tC[sl][:, 0:OWj],
                                             pbk[bk][:, 0:OWj], AF.Relu,
                                             bias=0.0, scale=1.0)), "ac")
                    ac = p.op("ac", (lambda e, sl=sl, OWj=OWj: e.activation(
                        tC[sl][:, 0:OWj], tC[sl][:, 0:OWj],
                        AF.Square, bias=0.0, scale=1.0)), "ac")
                    pbr[bk].note_rd("ac", ac)
                    tC_r[sl].note_wr("ac", ac)
                    p.wait("dv", "ac", ac)
                    hp = hidC_hi[:, m * CW + 1: m * CW + 1 + OWj]
                    lp = hidC_lo[:, m * CW + 1: m * CW + 1 + OWj]
                    p.op("dv", (lambda e, sl=sl, hp=hp, OWj=OWj:
                                e.tensor_copy(hp, tC[sl][:, 0:OWj])))
                    dv = p.op("dv", (lambda e, sl=sl, hp=hp, lp=lp, OWj=OWj:
                                     e.tensor_tensor(lp, tC[sl][:, 0:OWj], hp,
                                                     ALU.subtract)), "dv")
                    tC_r[sl].note_rd("dv", dv)
                    if m == 7:
                        hid_r.note_wr("dv", dv)
                mm_block(ring, OFFS["Wk"], 8, 8,
                         lambda k: pairC_hi[:, k * CW + 1: k * CW + 1 + OWj],
                         lambda k: pairC_lo[:, k * CW + 1: k * CW + 1 + OWj],
                         pairC_r, OWj, ev_relu2b)

            def c_wv(j):
                z0, ZW, lc0, CWj, OWj = cst(j)
                s5_r.wr("dv")
                s1_r.rd("dv")

                def ev_out(m, bk):
                    sl = m % 2
                    tC_r[sl].wr("dv")
                    pbr[bk].rd("dv")
                    p.op("dv", (lambda e, m=m, bk=bk, sl=sl, OWj=OWj:
                                e.scalar_tensor_tensor(
                                    tC[sl][:, 0:OWj],
                                    pbk[bk][:, 0:OWj], 0.5,
                                    s1[:, m * CW + 1: m * CW + 1 + OWj],
                                    ALU.mult, ALU.mult)))
                    dv = p.op("dv", (lambda e, m=m, sl=sl, OWj=OWj:
                                     e.tensor_tensor(
                                         s5[:, m * CW + 1: m * CW + 1 + OWj],
                                         s5[:, m * CW + 1: m * CW + 1 + OWj],
                                         tC[sl][:, 0:OWj], ALU.add)), "dv")
                    pbr[bk].note_rd("dv", dv)
                    tC_r[sl].note_wr("dv", dv)
                    tC_r[sl].note_rd("dv", dv)
                    if m == 7:
                        s1_r.note_rd("dv", dv)
                        s5_r.note_wr("dv", dv)
                mm_block(ring, OFFS["Wv"], 8, 8,
                         lambda k: hidC_hi[:, k * CW + 1: k * CW + 1 + OWj],
                         lambda k: hidC_lo[:, k * CW + 1: k * CW + 1 + OWj],
                         hid_r, OWj, ev_out)

                s5_r.rd("gp")
                st = p.op("gp", (lambda e, z0=z0, OWj=OWj: e.dma_start(
                    bass.AP(out_h, z0 + 1 - OUT0,
                            [[OUTW, 128], [128 * OUTW, 8], [1, OWj]]),
                    bass.AP(s5, 1, [[8 * CW, 128], [CW, 8], [1, OWj]]))),
                    "st", 16)
                s5_r.note_rd("st", st)

            NJ = NCT if "C" in PASSES else 0
            bcW, bcZ = {}, {}
            if NJ:
                c_head(0)
                c_up(0)
                bcW[0] = c_stats_w1s(0)
                bcW[0]()
                c_norm2(0)
            for j in range(NJ):
                c_ffn1(j)
                if j + 1 < NJ:
                    c_head(j + 1)
                c_ffn2(j)
                bcZ[j] = c_stats_ln3(j)
                if j + 1 < NJ:
                    c_up(j + 1)
                bcZ[j]()
                if j + 1 < NJ:
                    bcW[j + 1] = c_stats_w1s(j + 1)
                c_norm3r(j)
                c_wr(j)
                if j + 1 < NJ:
                    bcW[j + 1]()
                c_shiftk(j)
                c_wk(j)
                if j + 1 < NJ:
                    c_norm2(j + 1)
                c_wv(j)

            p.q["gp"].append(("w", "st", p.cnt["st"]))

        with nc.Block() as block:
            p.emit(block)
    return nc


def kernel(**inputs):
    in_maps = _host_pack(inputs)
    nc = build(DEBUG_TAPS)
    res = run_bass_kernel_spmd(nc, in_maps, core_ids=list(range(NCORE)))
    return _host_unpack(res.results)


# version 32
# speedup vs baseline: 1.0544x; 1.0544x over previous
import contextlib
import numpy as np
import concourse.bass as bass
import concourse.mybir as mybir
from concourse.bass_utils import run_bass_kernel_spmd

F32 = mybir.dt.float32
BF16 = mybir.dt.bfloat16
AF = mybir.ActivationFunctionType
ALU = mybir.AluOpType
AX = mybir.AxisListType

B, T, D = 4, 4096, 1024
NCORE = 8
COLS = 2080
NCOMP = 2076
OUT0, OUT1 = 20, 2076
OUTW = OUT1 - OUT0

ATW = 260
NA = COLS // ATW
AW = ATW + 1
CTW = 346
NCT = 6
CW = CTW + 1

SCALE = float(np.float32(32 ** (-0.5)))
NEG = -1.0e30

def _chunks(base, last_start, last_w):
    out = [(base + 120 * k, 120, 0 if k == 0 else 1) for k in range(17)]
    out.append((last_start, last_w, 2))
    return out

CHUNKS = {0: _chunks(15, 2055, 25), 1: _chunks(10, 2050, 30)}
BR_W = {0: 10, 1: 20}
BR_S = {0: 5, 1: 10}
CLS_W = {0: [120, 120, 25], 1: [120, 120, 30]}

NBLK = 128 * 128
NBLOB = (64 * 3 + 32 + 32 + 128 + 128) * NBLK
NPRE = 2 * (8 * NBLK + 2 * 128 * 256 + 4 * NBLK)
NTAB = 8 * (2 * 120 * 120 + 25 * 25) + 8 * (2 * 120 * 120 + 30 * 30)

DEBUG_TAPS = []
PASSES = "ABC"
BSTAGE = 9
BDUMP = -1

VECS = {
    "down_b": (0, 4), "norm1_w": (4, 4), "up_b": (8, 8), "norm2_w": (16, 8),
    "ffn_b1": (24, 16), "ffn_b2": (40, 8), "mu_r": (48, 8), "mu_k": (56, 8),
    "proj1_b": (64, 2), "proj2_b": (66, 2),
    "qkv1_b_qk": (68, 4), "qkv2_b_qk": (72, 4),
}
NV = 78
EPS5_COL, EPS6_COL = 76, 77

OFFS = {
    "Wr": 0, "Wk": 64 * NBLK, "Wv": 128 * NBLK, "down": 192 * NBLK,
    "up": 224 * NBLK, "ffn1": 256 * NBLK, "ffn2": 384 * NBLK,
}


def _pack_vecs(ins):
    v = np.zeros((128, NV), np.float32)

    def put(name, arr):
        c0, n = VECS[name]
        v[:, c0:c0 + n] = np.asarray(arr, np.float32).reshape(n, 128).T

    put("down_b", ins["L0_attn_down_b"])
    put("norm1_w", ins["L0_attn_norm1_w"])
    put("up_b", ins["L0_attn_up_b"])
    put("norm2_w", ins["L0_attn_norm2_w"])
    put("ffn_b1", ins["L0_attn_ffn_b1"])
    put("ffn_b2", ins["L0_attn_ffn_b2"])
    put("mu_r", ins["L0_ffn_mu_r"])
    put("mu_k", ins["L0_ffn_mu_k"])
    put("proj1_b", ins["L0_attn_proj1_b"])
    put("proj2_b", ins["L0_attn_proj2_b"])
    put("qkv1_b_qk", np.asarray(ins["L0_attn_qkv1_b"])[0:512])
    put("qkv2_b_qk", np.asarray(ins["L0_attn_qkv2_b"])[0:512])
    v[:, EPS5_COL] = 1e-5
    v[:, EPS6_COL] = 1e-6
    return v


def _blob(ins):
    parts = []

    def addmat(lhsT, nm, nk):
        for m in range(nm):
            for k in range(nk):
                parts.append(np.ascontiguousarray(
                    lhsT[k * 128:(k + 1) * 128, m * 128:(m + 1) * 128],
                    np.float32).reshape(-1))

    addmat(np.asarray(ins["L0_ffn_W_r"], np.float32).T, 8, 8)
    addmat(np.asarray(ins["L0_ffn_W_k"], np.float32).T, 8, 8)
    addmat(np.asarray(ins["L0_ffn_W_v"], np.float32).T, 8, 8)
    addmat(np.asarray(ins["L0_attn_down_w"], np.float32), 4, 8)
    addmat(np.asarray(ins["L0_attn_up_w"], np.float32), 8, 4)
    addmat(np.asarray(ins["L0_attn_ffn_w1"], np.float32), 16, 8)
    addmat(np.asarray(ins["L0_attn_ffn_w2"], np.float32), 8, 16)
    out = np.concatenate(parts)
    assert out.size == NBLOB, out.size
    return out


def _pre(ins):
    parts = []
    for bi in (0, 1):
        qkv = np.asarray(ins[f"L0_attn_qkv{bi+1}_w"], np.float32)
        pj = np.asarray(ins[f"L0_attn_proj{bi+1}_w"], np.float32)
        for m in range(4):
            for k in range(2):
                parts.append(np.ascontiguousarray(
                    qkv[k * 128:(k + 1) * 128, m * 128:(m + 1) * 128]).reshape(-1))
        for k in range(2):
            parts.append(np.ascontiguousarray(
                qkv[k * 128:(k + 1) * 128, 512:768]).reshape(-1))
        for m in range(2):
            for k in range(2):
                parts.append(np.ascontiguousarray(
                    pj[k * 128:(k + 1) * 128, m * 128:(m + 1) * 128]).reshape(-1))
    out = np.concatenate(parts)
    assert out.size == NPRE, out.size
    return out


def _chunk_table(rpb, W, S, c0, s, C):
    toks = np.arange(s, s + C) + c0
    code = np.full(C, -100, np.int64)
    lo = np.zeros(C, np.int64)
    hi = np.zeros(C, np.int64)
    for i, t in enumerate(toks):
        if t < 0 or t >= 4100:
            continue
        if t < S:
            code[i], lo[i], hi[i] = -2, 0, S
        elif t >= 4100 - W + S:
            code[i], lo[i], hi[i] = -3, 4100 - W + S, 4100
        else:
            k = (t - S) // W
            code[i], lo[i], hi[i] = k, S + W * k, S + W * (k + 1)
    live = (code > -100) & (lo - c0 >= s) & (hi - c0 <= s + C)
    same = (code[:, None] == code[None, :]) & live[:, None] & (code[:, None] > -100)
    dmat = np.clip(toks[:, None] - toks[None, :] + W - 1, 0, 2 * W - 2)
    bias = rpb[dmat, :]
    tab = np.where(same[:, :, None], bias, np.float32(NEG))
    return np.ascontiguousarray(tab.transpose(2, 0, 1).astype(np.float32))


def _tables(ins, half):
    c0 = -20 if half == 0 else 2020
    out = {}
    for bi in (0, 1):
        W, S = BR_W[bi], BR_S[bi]
        rpb = np.asarray(ins[f"L0_attn_rpb{bi+1}"], np.float32)
        for cls, idx in ((0, 0), (1, 1), (2, 17)):
            s, C, _ = CHUNKS[bi][idx]
            out[(bi, cls)] = _chunk_table(rpb, W, S, c0, s, C)
    return out


def _host_pack(ins):
    x = np.asarray(ins["L0_attn_x"], np.float32)
    vecs = _pack_vecs(ins)
    blob = _blob(ins)
    pre = _pre(ins)
    rows = np.concatenate([
        np.asarray(ins["L0_attn_qkv1_b"], np.float32)[512:768],
        np.asarray(ins["L0_attn_qkv2_b"], np.float32)[512:768]])[None, :]
    ident = np.eye(128, dtype=np.float32)
    in_maps = []
    for c in range(NCORE):
        b, h = c // 2, c % 2
        c0 = -20 if h == 0 else 2020
        xs = np.zeros((COLS, D), np.float32)
        lo, hi = max(0, c0), min(T, c0 + COLS)
        xs[lo - c0:hi - c0] = x[b, lo:hi]
        xfm = np.ascontiguousarray(xs.T.reshape(8, 128, COLS))
        tabs = _tables(ins, h)
        atab = np.concatenate([tabs[(bi, cls)].reshape(-1)
                               for bi in (0, 1) for cls in (0, 1, 2)])
        assert atab.size == NTAB
        in_maps.append({"x_in": xfm, "wblob": blob, "wpre": pre, "vecs": vecs,
                        "rows": rows, "ident": ident, "atab": atab})
    return in_maps


def _host_unpack(results):
    out = np.empty((B, T, D), np.float32)
    for c in range(NCORE):
        b, h = c // 2, c % 2
        o = results[c]["out"].reshape(1024, OUTW)
        sl = o[:, 0:2048] if h == 0 else o[:, 8:2056]
        out[b, h * 2048:(h + 1) * 2048, :] = sl.T
    return out


ENGINES = ("sp", "pe", "dv", "ac", "gp")
ENG_SEM = {"pe": "pe", "dv": "dv", "ac": "ac", "gp": "gp"}


class P:
    def __init__(self):
        self.q = {e: [] for e in ENGINES}
        self.cnt = {}
        self.hwm = {e: {} for e in ENGINES}
        self.semh = {}

    def reg_sem(self, name, handle):
        self.semh[name] = handle
        self.cnt[name] = 0
        for e in ENGINES:
            self.hwm[e][name] = 0

    def wait(self, eng, sem, val):
        if val is None or val <= 0:
            return
        if ENG_SEM.get(eng) == sem:
            return
        if self.hwm[eng][sem] >= val:
            return
        self.hwm[eng][sem] = val
        self.q[eng].append(("w", sem, val))

    def op(self, eng, fn, sem=None, n=1):
        self.q[eng].append(("o", fn, sem, n))
        if sem is not None:
            self.cnt[sem] += n
            return self.cnt[sem]
        return None

    def flush(self, eng):
        sem = ENG_SEM[eng]
        self.q[eng].append(("w", sem, self.cnt[sem]))

    def fence(self):
        caps = [(s, self.cnt[s]) for s in ("pe", "dv", "ac", "gp", "ld", "st")]
        for e in ENGINES:
            for s, v in caps:
                self.wait(e, s, v)

    def emit(self, block):
        def mk(items):
            def run(eng):
                for it in items:
                    if it[0] == "w":
                        eng.wait_ge(self.semh[it[1]], it[2])
                    else:
                        inst = it[1](eng)
                        if it[2] is not None:
                            inst.then_inc(self.semh[it[2]], it[3])
            return run
        block.sync(mk(self.q["sp"]))
        block.tensor(mk(self.q["pe"]))
        block.vector(mk(self.q["dv"]))
        block.scalar(mk(self.q["ac"]))
        block.gpsimd(mk(self.q["gp"]))


class Res:
    def __init__(self, p):
        self.p, self.w, self.r = p, None, []

    def rd(self, eng):
        if self.w:
            self.p.wait(eng, *self.w)

    def note_rd(self, sem, cnt):
        self.r.append((sem, cnt))

    def wr(self, eng):
        for s, c in self.r:
            self.p.wait(eng, s, c)
        if self.w:
            self.p.wait(eng, *self.w)
        self.r = []

    def note_wr(self, sem, cnt):
        self.w = (sem, cnt)


class Ring:
    def __init__(self, p, sb, wsplit_h, split_ready, nslots=48):
        self.p, self.sb, self.ws, self.ns = p, sb, wsplit_h, nslots
        self.split_ready = split_ready
        self.cur = 0
        self.slot_note = [None] * nslots
        self.pending = []

    def load_pair_group(self, off_f32, nk):
        nb = 2 * nk
        s0 = self.cur % self.ns
        assert s0 + nb <= self.ns
        gate = self.split_ready(off_f32 + nk * NBLK)
        if gate:
            self.p.wait("sp", *gate)
        seen = set()
        for s in range(s0, s0 + nb):
            note = self.slot_note[s]
            if note and id(note) not in seen:
                seen.add(id(note))
                self.p.wait("sp", *note)
        sb, ws, ns = self.sb, self.ws, self.ns

        def fn(eng, s0=s0, nb=nb, off=2 * off_f32):
            return eng.dma_start(
                bass.AP(sb, s0 * 128, [[ns * 128, 128], [128, nb], [1, 128]]),
                bass.AP(ws, off, [[128, 128], [NBLK, nb], [1, 128]]))
        cnt = self.p.op("sp", fn, "ld", 16)
        self.pending.append((s0, nb, cnt))
        self.cur += nb
        return s0

    def use_group(self):
        s0, nb, cnt = self.pending[0]
        self.p.wait("pe", "ld", cnt)
        return s0

    def done_group(self, pe_cnt):
        s0, nb, _ = self.pending.pop(0)
        note = ("pe", pe_cnt)
        for s in range(s0, s0 + nb):
            self.slot_note[s] = note

    def blk(self, s0, i):
        return self.sb[:, (s0 + i) * 128:(s0 + i + 1) * 128]


def build(debug_taps=()):
    nc = bass.Bass("TRN2", target_bir_lowering=False)
    p = P()

    x_h = nc.dram_tensor("x_in", [8, 128, COLS], F32, kind="ExternalInput")
    blob_h = nc.dram_tensor("wblob", [NBLOB], F32, kind="ExternalInput")
    pre_h = nc.dram_tensor("wpre", [NPRE], F32, kind="ExternalInput")
    vecs_h = nc.dram_tensor("vecs", [128, NV], F32, kind="ExternalInput")
    rows_h = nc.dram_tensor("rows", [1, 512], F32, kind="ExternalInput")
    ident_h = nc.dram_tensor("ident", [128, 128], F32, kind="ExternalInput")
    atab_h = nc.dram_tensor("atab", [NTAB], F32, kind="ExternalInput")
    out_h = nc.dram_tensor("out", [8, 128, OUTW], F32, kind="ExternalOutput")
    vstat_h = nc.dram_tensor("vstat", [2, COLS], F32)
    wsplit_h = nc.dram_tensor("wsplit", [2 * NBLOB], BF16)
    taps = {}
    for t in debug_taps:
        shape = {"y1": [8, 128, COLS], "xi": [4, 128, COLS],
                 "o": [4, 128, COLS]}[t]
        taps[t] = nc.dram_tensor("tap_" + t, shape, F32, kind="ExternalOutput")
    bdump_h = {}
    if BDUMP >= 0:
        for nm_, w_ in (("qk", 512), ("vT", 256), ("S", 1024), ("P", 1024),
                        ("O", 256), ("nmx", 8), ("sms", 8)):
            bdump_h[nm_] = nc.dram_tensor("bd_" + nm_, [128, w_], F32,
                                          kind="ExternalOutput")

    with contextlib.ExitStack() as ctx:
        def sb(name, shape, stack=None, dt=F32):
            return (stack or ctx).enter_context(nc.sbuf_tensor(name, shape, dt))

        y1 = sb("y1", [128, 8 * COLS])
        o_full = sb("o_full", [128, 4 * COLS])
        vecs = sb("vecs_s", [128, NV])
        ones1 = sb("ones1", [1, 128])
        inv1024 = sb("inv1024", [128, 1])
        inv512 = sb("inv512", [128, 1])
        pbk = [ctx.enter_context(nc.psum_tensor(f"pb{i}", [128, 512], F32))
               for i in range(8)]
        pbr = [Res(p) for _ in range(8)]
        for s in ("ld", "st", "pe", "dv", "ac", "gp"):
            p.reg_sem(s, ctx.enter_context(nc.semaphore(s)))

        y1_r, o_r = Res(p), Res(p)
        vecs_r = Res(p)

        def dma(dst_ap, src_ap):
            return lambda e, d=dst_ap, s=src_ap: e.dma_start(d, s)

        ld0 = p.op("sp", dma(vecs[:, :], vecs_h[:, :]), "ld", 16)
        vecs_r.note_wr("ld", ld0)
        p.op("dv", lambda e: e.memset(ones1[:, :], 1.0), "dv")
        p.op("dv", lambda e: e.memset(inv1024[:, :], 1.0 / 1024.0), "dv")
        dv_c = p.op("dv", lambda e: e.memset(inv512[:, :], 1.0 / 512.0), "dv")
        consts_ok = ("dv", dv_c)

        def stats(src, src_r, nch, sstride, scoff, W, inv, eps, want_m,
                  sqs, sq_rs, rA, rB, rC, rows_r, bks=(2, 3, 4, 5),
                  defer_bcast=False):
            b_sum, b_sq, b_rs, b_mrs = bks
            pbr[b_sum].wr("pe")
            pbr[b_sq].wr("pe")
            p.wait("pe", *consts_ok)
            for c in range(nch):
                s_ap = src[:, c * sstride + scoff: c * sstride + scoff + W]
                slot = c % 2
                sq, sq_r = sqs[slot], sq_rs[slot]
                sq_r.wr("ac")
                src_r.rd("ac")
                cnt = p.op("ac", (lambda e, a=s_ap, W=W, sq=sq:
                                  e.activation(sq[:, 0:W], a, AF.Square,
                                               bias=0.0, scale=1.0)), "ac")
                sq_r.note_wr("ac", cnt)
                src_r.note_rd("ac", cnt)
                src_r.rd("pe")
                p.op("pe", (lambda e, a=s_ap, st=(c == 0), sp=(c == nch - 1),
                            W=W, inv=inv, b=b_sum: e.matmul(
                                pbk[b][0:1, 0:W], inv, a, start=st, stop=sp)))
                sq_r.rd("pe")
                pe = p.op("pe", (lambda e, st=(c == 0), sp=(c == nch - 1),
                                 W=W, inv=inv, sq=sq, b=b_sq: e.matmul(
                                     pbk[b][0:1, 0:W], inv, sq[:, 0:W],
                                     start=st, stop=sp)), "pe")
                sq_r.note_rd("pe", pe)
            src_r.note_rd("pe", pe)
            pbr[b_sum].note_wr("pe", pe)
            pbr[b_sq].note_wr("pe", pe)
            rows_r.wr("dv")
            if want_m:
                pbr[b_sum].rd("dv")
                p.op("dv", (lambda e, W=W, b=b_sum: e.tensor_copy(
                    rA[0:1, 0:W], pbk[b][0:1, 0:W])))
                p.op("dv", (lambda e, W=W: e.tensor_tensor(
                    rB[0:1, 0:W], rA[0:1, 0:W], rA[0:1, 0:W], ALU.mult)))
                pbr[b_sq].rd("dv")
                dv = p.op("dv", (lambda e, W=W, b=b_sq: e.tensor_tensor(
                    rC[0:1, 0:W], pbk[b][0:1, 0:W], rB[0:1, 0:W],
                    ALU.subtract)), "dv")
                pbr[b_sum].note_rd("dv", dv)
                pbr[b_sq].note_rd("dv", dv)
                p.wait("ac", "dv", dv)
                ac = p.op("ac", (lambda e, W=W, eps=eps: e.activation(
                    rB[0:1, 0:W], rC[0:1, 0:W], AF.Sqrt,
                    bias=vecs[0:1, eps:eps + 1], scale=1.0)), "ac")
                p.wait("dv", "ac", ac)
                p.op("dv", (lambda e, W=W: e.reciprocal(
                    rB[0:1, 0:W], rB[0:1, 0:W])))
                dv = p.op("dv", (lambda e, W=W: e.tensor_tensor(
                    rC[0:1, 0:W], rA[0:1, 0:W], rB[0:1, 0:W], ALU.mult)), "dv")
                rows_r.note_wr("dv", dv)
            else:
                pbr[b_sq].rd("ac")
                ac = p.op("ac", (lambda e, W=W, eps=eps, b=b_sq: e.activation(
                    rB[0:1, 0:W], pbk[b][0:1, 0:W], AF.Sqrt,
                    bias=vecs[0:1, eps:eps + 1], scale=1.0)), "ac")
                pbr[b_sq].note_rd("ac", ac)
                p.wait("dv", "ac", ac)
                dv = p.op("dv", (lambda e, W=W: e.reciprocal(
                    rB[0:1, 0:W], rB[0:1, 0:W])), "dv")
                rows_r.note_wr("dv", dv)
            def do_bcast():
                pbr[b_rs].wr("pe")
                rows_r.rd("pe")
                pe = p.op("pe", (lambda e, W=W, b=b_rs: e.matmul(
                    pbk[b][:, 0:W], ones1[0:1, :], rB[0:1, 0:W],
                    start=True, stop=True)), "pe")
                pbr[b_rs].note_wr("pe", pe)
                rows_r.note_rd("pe", pe)
                if want_m:
                    pbr[b_mrs].wr("pe")
                    pe = p.op("pe", (lambda e, W=W, b=b_mrs: e.matmul(
                        pbk[b][:, 0:W], ones1[0:1, :], rC[0:1, 0:W],
                        start=True, stop=True)), "pe")
                    pbr[b_mrs].note_wr("pe", pe)
                    rows_r.note_rd("pe", pe)
            if defer_bcast:
                return do_bcast
            do_bcast()

        def normalize(src, src_r, dst, dst_r, nch, sstride, scoff,
                      dstride, dcoff, W, b_rs=4, b_mrs=5):
            dst_r.wr("dv")
            pbr[b_rs].rd("dv")
            pbr[b_mrs].rd("dv")
            src_r.rd("dv")
            for c in range(nch):
                s_ap = src[:, c * sstride + scoff: c * sstride + scoff + W]
                d_ap = dst[:, c * dstride + dcoff: c * dstride + dcoff + W]
                p.op("dv", (lambda e, s=s_ap, d=d_ap, W=W, b=b_rs:
                            e.tensor_tensor(d, s, pbk[b][:, 0:W], ALU.mult)))
                dv = p.op("dv", (lambda e, d=d_ap, W=W, b=b_mrs:
                                 e.tensor_tensor(d, d, pbk[b][:, 0:W],
                                                 ALU.subtract)), "dv")
            dst_r.note_wr("dv", dv)
            src_r.note_rd("dv", dv)
            pbr[b_rs].note_rd("dv", dv)
            pbr[b_mrs].note_rd("dv", dv)

        def shiftmix(u, u_r, xr, xr_r, stride, W, mu_col):
            xr_r.wr("dv")
            u_r.rd("dv")
            vecs_r.rd("dv")
            d3o = xr[:, :].rearrange("p (c t) -> p c t", c=8)[:, :, 1:W + 1]
            u3a = u[:, :].rearrange("p (c t) -> p c t", c=8)[:, :, 1:W + 1]
            u3b = u[:, :].rearrange("p (c t) -> p c t", c=8)[:, :, 0:W]
            p.op("dv", (lambda e, a=d3o, b=u3a, c=u3b: e.tensor_tensor(
                a, b, c, ALU.subtract)))
            for c in range(8):
                x_ap = xr[:, c * stride + 1: c * stride + 1 + W]
                u_ap = u[:, c * stride: c * stride + W]
                dv = p.op("dv", (lambda e, x=x_ap, u_=u_ap, c_=c:
                                 e.scalar_tensor_tensor(
                                     x, x, vecs[:, mu_col + c_:mu_col + c_ + 1],
                                     u_, ALU.mult, ALU.add)), "dv")
            xr_r.note_wr("dv", dv)
            u_r.note_rd("dv", dv)

        def mm_block(ring, offs, nm_groups, nk, rhs_hi, rhs_lo, rhs_r,
                     W, evac):
            n3 = 3 * nk
            for m in range(nm_groups):
                ring.load_pair_group(offs + m * nk * NBLK, nk)
                bk = m % 4
                s0 = ring.use_group()
                pbr[bk].wr("pe")
                notes = getattr(rhs_r, "chunk_notes", None)
                if notes is None or len(notes) < nk:
                    rhs_r.rd("pe")
                i3 = 0
                for k in range(nk):
                    if notes is not None and len(notes) >= nk:
                        p.wait("pe", "dv", notes[k])
                    whi, wlo = ring.blk(s0, 2 * k), ring.blk(s0, 2 * k + 1)
                    for wap, aap in ((whi, rhs_hi(k)), (whi, rhs_lo(k)),
                                     (wlo, rhs_hi(k))):
                        i3 += 1
                        pe = p.op("pe", (lambda e, wap=wap, aap=aap, bk=bk,
                                         W=W, st=(i3 == 1), sp=(i3 == n3):
                                         e.matmul(pbk[bk][:, 0:W], wap, aap,
                                                  start=st, stop=sp)),
                                  "pe" if i3 == n3 else None)
                ring.done_group(pe)
                pbr[bk].note_wr("pe", pe)
                if m == nm_groups - 1:
                    rhs_r.note_rd("pe", pe)
                evac(m, bk)

        def split_pair(src, src_r, hi, lo, pair_r, nch, sstride, scoff,
                      pstride, pcoff, W):
            pair_r.wr("dv")
            src_r.rd("dv")
            pair_r.chunk_notes = []
            for c in range(nch):
                s_ap = src[:, c * sstride + scoff: c * sstride + scoff + W]
                h_ap = hi[:, c * pstride + pcoff: c * pstride + pcoff + W]
                l_ap = lo[:, c * pstride + pcoff: c * pstride + pcoff + W]
                p.op("dv", (lambda e, s=s_ap, h=h_ap: e.tensor_copy(h, s)))
                dv = p.op("dv", (lambda e, s=s_ap, h=h_ap, l=l_ap:
                                 e.tensor_tensor(l, s, h, ALU.subtract)), "dv")
                pair_r.chunk_notes.append(dv)
            pair_r.note_wr("dv", dv)
            src_r.note_rd("dv", dv)

        with contextlib.ExitStack() as s1x:
            xi = sb("xi", [128, 4 * COLS], s1x)
            xi_r = Res(p)

            with contextlib.ExitStack() as wx:
                SPLC = 8 * NBLK
                NSPL = NBLOB // SPLC
                w_in = [wx.enter_context(nc.sbuf_tensor(
                    f"w_in{i}", [128, 1024], F32)) for i in range(2)]
                w_hi = [wx.enter_context(nc.sbuf_tensor(
                    f"w_hi{i}", [128, 1024], BF16)) for i in range(2)]
                w_lo = [wx.enter_context(nc.sbuf_tensor(
                    f"w_lo{i}", [128, 1024], BF16)) for i in range(2)]
                win_r = [Res(p), Res(p)]
                wp_r = [Res(p), Res(p)]
                for i_ in range(NSPL if "A" in PASSES or "C" in PASSES else 0):
                    off = i_ * SPLC
                    sl = i_ % 2
                    eng = "dv" if i_ % 2 == 0 else "gp"
                    esem = "dv" if eng == "dv" else "gp"
                    win_r[sl].wr("sp")
                    ldg = p.op("sp", (lambda e, off=off, sl=sl: e.dma_start(
                        bass.AP(w_in[sl], 0, [[1024, 128], [128, 8], [1, 128]]),
                        bass.AP(blob_h, off, [[128, 128], [NBLK, 8],
                                              [1, 128]]))), "ld", 16)
                    win_r[sl].note_wr("ld", ldg)
                    wp_r[sl].wr(eng)
                    win_r[sl].rd(eng)
                    p.op(eng, (lambda e, sl=sl: e.tensor_copy(
                        w_hi[sl][:, :], w_in[sl][:, :])))
                    cst = p.op(eng, (lambda e, sl=sl: e.tensor_tensor(
                        w_lo[sl][:, :], w_in[sl][:, :], w_hi[sl][:, :],
                        ALU.subtract)), esem)
                    wp_r[sl].note_wr(esem, cst)
                    win_r[sl].note_rd(esem, cst)
                    wp_r[sl].rd("gp")
                    p.op("gp", (lambda e, off=off, sl=sl: e.dma_start(
                        bass.AP(wsplit_h, 2 * off,
                                [[128, 128], [2 * NBLK, 8], [1, 128]]),
                        bass.AP(w_hi[sl], 0, [[1024, 128], [128, 8],
                                              [1, 128]]))), "st", 16)
                    stc = p.op("gp", (lambda e, off=off, sl=sl: e.dma_start(
                        bass.AP(wsplit_h, 2 * off + NBLK,
                                [[128, 128], [2 * NBLK, 8], [1, 128]]),
                        bass.AP(w_lo[sl], 0, [[1024, 128], [128, 8],
                                              [1, 128]]))), "st", 16)
                    wp_r[sl].note_rd("st", stc)
            p.fence()

            def split_ready(off_f32_end):
                return None

            with contextlib.ExitStack() as ax:
                pass

                ring = Ring(p, sb("ringA", [128, 32 * 128], ax, BF16),
                            wsplit_h, split_ready, nslots=32)
                pA_hi = sb("pA_hi", [128, 8 * AW], ax, BF16)
                pA_lo = sb("pA_lo", [128, 8 * AW], ax, BF16)
                pA_r = Res(p)
                pB_hi = sb("pB_hi", [128, 8 * AW], ax, BF16)
                pB_lo = sb("pB_lo", [128, 8 * AW], ax, BF16)
                pB_r = Res(p)
                tks = [sb(f"tkA{i}", [128, AW], ax) for i in range(2)]
                tk_rs = [Res(p), Res(p)]
                xb = [sb(f"xA{i}", [128, 8 * AW], ax) for i in range(2)]
                xb_r = [Res(p), Res(p)]
                ub = sb("uA", [128, 8 * AW], ax)
                ub_r = Res(p)
                xrb = sb("xrA", [128, 8 * AW], ax)
                xrb_r = Res(p)
                rb = sb("rA", [128, 8 * AW], ax)
                rb_r = Res(p)
                sqs = [sb(f"sqA{i}", [128, AW], ax) for i in range(2)]
                sq_rs = [Res(p), Res(p)]
                rA_ = sb("rowA", [1, AW], ax)
                rB_ = sb("rowB", [1, AW], ax)
                rC_ = sb("rowC", [1, AW], ax)
                rows_r = Res(p)
                rD_ = sb("rowD", [1, AW], ax)
                rE_ = sb("rowE", [1, AW], ax)
                rF_ = sb("rowF", [1, AW], ax)
                rows_r2 = Res(p)

                def ranges(j):
                    o0 = j * ATW
                    return dict(
                        o0=o0, oc0=o0 + (1 if j == 0 else 0),
                        u0=o0 - 1 if j > 0 else 0,
                        OW=ATW - (1 if j == 0 else 0))

                def head(j):
                    r = ranges(j)
                    u0, OW = r["u0"], r["OW"]
                    UW = OW + 1
                    par = j % 2
                    xt, xt_r = xb[par], xb_r[par]
                    xt_r.wr("sp")
                    ld = p.op("sp", (lambda e, u0=u0, UW=UW, xt=xt:
                                     e.dma_start(
                                         bass.AP(xt, 0, [[8 * AW, 128],
                                                 [AW, 8], [1, UW]]),
                                         bass.AP(x_h, u0, [[COLS, 128],
                                                 [128 * COLS, 8], [1, UW]]))),
                             "ld", 16)
                    xt_r.note_wr("ld", ld)
                    stats(xt, xt_r, 8, AW, 0, UW, inv1024[:, :], EPS5_COL,
                          True, sqs, sq_rs, rD_, rE_, rF_, rows_r2,
                          bks=(6, 7, 6, 7), defer_bcast=True)

                def prep(j):
                    r = ranges(j)
                    OW = r["OW"]
                    par = j % 2
                    xt, xt_r = xb[par], xb_r[par]
                    normalize(xt, xt_r, ub, ub_r, 8, AW, 0, AW, 0, OW + 1,
                              b_rs=6, b_mrs=7)
                    shiftmix(ub, ub_r, xrb, xrb_r, AW, OW, VECS["mu_r"][0])
                    split_pair(xrb, xrb_r, pA_hi, pA_lo, pA_r,
                               8, AW, 1, AW, 1, OW)

                def body(j):
                    r = ranges(j)
                    oc0, OW = r["oc0"], r["OW"]
                    par = j % 2
                    xt, xt_r = xb[par], xb_r[par]

                    rb_r.wr("ac")

                    def ev_sig(m, bk):
                        pbr[bk].rd("ac")
                        ac = p.op("ac", (lambda e, m=m, bk=bk, OW=OW:
                                         e.activation(
                                             rb[:, m * AW + 1:
                                                m * AW + 1 + OW],
                                             pbk[bk][:, 0:OW], AF.Sigmoid,
                                             bias=0.0, scale=1.0)), "ac")
                        pbr[bk].note_rd("ac", ac)
                        if m == 7:
                            rb_r.note_wr("ac", ac)
                    mm_block(ring, OFFS["Wr"], 8, 8,
                             lambda k: pA_hi[:, k * AW + 1: k * AW + 1 + OW],
                             lambda k: pA_lo[:, k * AW + 1: k * AW + 1 + OW],
                             pA_r, OW, ev_sig)

                    shiftmix(ub, ub_r, xrb, xrb_r, AW, OW, VECS["mu_k"][0])
                    split_pair(xrb, xrb_r, pB_hi, pB_lo, pB_r,
                               8, AW, 1, AW, 1, OW)

                    pA_r.wr("dv")
                    pA_r.chunk_notes = []

                    def ev_relu2(m, bk):
                        sl = m % 2
                        tk, tk_r = tks[sl], tk_rs[sl]
                        tk_r.wr("ac")
                        pbr[bk].rd("ac")
                        p.op("ac", (lambda e, bk=bk, tk=tk, OW=OW:
                                    e.activation(
                                        tk[:, 0:OW], pbk[bk][:, 0:OW],
                                        AF.Relu, bias=0.0, scale=1.0)), "ac")
                        ac = p.op("ac", (lambda e, tk=tk, OW=OW: e.activation(
                            tk[:, 0:OW], tk[:, 0:OW], AF.Square,
                            bias=0.0, scale=1.0)), "ac")
                        pbr[bk].note_rd("ac", ac)
                        tk_r.note_wr("ac", ac)
                        p.wait("dv", "ac", ac)
                        hp = pA_hi[:, m * AW + 1: m * AW + 1 + OW]
                        lp = pA_lo[:, m * AW + 1: m * AW + 1 + OW]
                        p.op("dv", (lambda e, hp=hp, tk=tk, OW=OW:
                                    e.tensor_copy(hp, tk[:, 0:OW])))
                        dv = p.op("dv", (lambda e, hp=hp, lp=lp, tk=tk,
                                         OW=OW: e.tensor_tensor(
                                             lp, tk[:, 0:OW], hp,
                                             ALU.subtract)), "dv")
                        tk_r.note_rd("dv", dv)
                        pA_r.chunk_notes.append(dv)
                        if m == 7:
                            pA_r.note_wr("dv", dv)
                    mm_block(ring, OFFS["Wk"], 8, 8,
                             lambda k: pB_hi[:, k * AW + 1: k * AW + 1 + OW],
                             lambda k: pB_lo[:, k * AW + 1: k * AW + 1 + OW],
                             pB_r, OW, ev_relu2)

                    if j + 1 < NA:
                        head(j + 1)

                    def ev_y1(m, bk):
                        pbr[bk].rd("dv")
                        rb_r.rd("dv")
                        y1_r.wr("dv")
                        xt_r.rd("dv")
                        dst = y1[:, m * COLS + oc0: m * COLS + oc0 + OW]
                        p.op("dv", (lambda e, m=m, bk=bk, dst=dst, OW=OW:
                                    e.scalar_tensor_tensor(
                                        dst, pbk[bk][:, 0:OW], 0.5,
                                        rb[:, m * AW + 1: m * AW + 1 + OW],
                                        ALU.mult, ALU.mult)))
                        dv = p.op("dv", (lambda e, dst=dst, OW=OW, xt=xt,
                                         m=m: e.tensor_tensor(
                                             dst, dst,
                                             xt[:, m * AW + 1:
                                                m * AW + 1 + OW],
                                             ALU.add)), "dv")
                        pbr[bk].note_rd("dv", dv)
                        y1_r.note_wr("dv", dv)
                        if m == 7:
                            rb_r.note_rd("dv", dv)
                            xt_r.note_rd("dv", dv)
                    mm_block(ring, OFFS["Wv"], 8, 8,
                             lambda k: pA_hi[:, k * AW + 1: k * AW + 1 + OW],
                             lambda k: pA_lo[:, k * AW + 1: k * AW + 1 + OW],
                             pA_r, OW, ev_y1)

                    if j + 1 < NA:
                        sr2 = rows_r2
                        pbr[6].wr("pe")
                        sr2.rd("pe")
                        pe = p.op("pe", (lambda e, W=ranges(j + 1)["OW"] + 1:
                                         e.matmul(pbk[6][:, 0:W],
                                                  ones1[0:1, :],
                                                  rE_[0:1, 0:W],
                                                  start=True, stop=True)),
                                  "pe")
                        pbr[6].note_wr("pe", pe)
                        sr2.note_rd("pe", pe)
                        pbr[7].wr("pe")
                        pe = p.op("pe", (lambda e, W=ranges(j + 1)["OW"] + 1:
                                         e.matmul(pbk[7][:, 0:W],
                                                  ones1[0:1, :],
                                                  rF_[0:1, 0:W],
                                                  start=True, stop=True)),
                                  "pe")
                        pbr[7].note_wr("pe", pe)
                        sr2.note_rd("pe", pe)
                    stats(y1, y1_r, 8, COLS, oc0, OW, inv1024[:, :],
                          EPS5_COL, True, sqs, sq_rs, rA_, rB_, rC_, rows_r)
                    rows_r.rd("sp")
                    st = p.op("sp", (lambda e, oc0=oc0, OW=OW: e.dma_start(
                        vstat_h[0:1, oc0:oc0 + OW], rB_[0:1, 0:OW])),
                        "ld", 16)
                    st = p.op("sp", (lambda e, oc0=oc0, OW=OW: e.dma_start(
                        vstat_h[1:2, oc0:oc0 + OW], rC_[0:1, 0:OW])),
                        "ld", 16)
                    rows_r.note_rd("ld", st)

                    vb, vb_r = xt, xt_r
                    normalize(y1, y1_r, vb, vb_r, 8, COLS, oc0, AW, 0, OW)
                    split_pair(vb, vb_r, pB_hi, pB_lo, pB_r,
                               8, AW, 0, AW, 0, OW)

                    if j + 1 < NA:
                        prep(j + 1)

                    rb_r.wr("dv")

                    def ev_down(m, bk):
                        pbr[bk].rd("ac")
                        ac = p.op("ac", (lambda e, m=m, bk=bk, OW=OW:
                                         e.activation(
                                             rb[:, m * AW: m * AW + OW],
                                             pbk[bk][:, 0:OW], AF.Identity,
                                             bias=vecs[:, VECS["down_b"][0] + m:
                                                       VECS["down_b"][0] + m + 1],
                                             scale=1.0)), "ac")
                        pbr[bk].note_rd("ac", ac)
                        if m == 3:
                            rb_r.note_wr("ac", ac)
                    mm_block(ring, OFFS["down"], 4, 8,
                             lambda k: pB_hi[:, k * AW: k * AW + OW],
                             lambda k: pB_lo[:, k * AW: k * AW + OW],
                             pB_r, OW, ev_down)
                    stats(rb, rb_r, 4, AW, 0, OW, inv512[:, :], EPS6_COL,
                          False, sqs, sq_rs, rA_, rB_, rC_, rows_r)
                    xi_r.wr("dv")
                    pbr[4].rd("dv")
                    for m in range(4):
                        dv = p.op("dv", (lambda e, m=m, OW=OW, oc0=oc0:
                                         e.scalar_tensor_tensor(
                                             xi[:, m * COLS + oc0:
                                                m * COLS + oc0 + OW],
                                             rb[:, m * AW: m * AW + OW],
                                             vecs[:, VECS["norm1_w"][0] + m:
                                                  VECS["norm1_w"][0] + m + 1],
                                             pbk[4][:, 0:OW],
                                             ALU.mult, ALU.mult)), "dv")
                    xi_r.note_wr("dv", dv)
                    rb_r.note_rd("dv", dv)
                    pbr[4].note_rd("dv", dv)

                if "A" in PASSES:
                    head(0)
                    pbr[6].wr("pe")
                    rows_r2.rd("pe")
                    pe = p.op("pe", (lambda e, W=ranges(0)["OW"] + 1:
                                     e.matmul(pbk[6][:, 0:W], ones1[0:1, :],
                                              rE_[0:1, 0:W],
                                              start=True, stop=True)), "pe")
                    pbr[6].note_wr("pe", pe)
                    rows_r2.note_rd("pe", pe)
                    pbr[7].wr("pe")
                    pe = p.op("pe", (lambda e, W=ranges(0)["OW"] + 1:
                                     e.matmul(pbk[7][:, 0:W], ones1[0:1, :],
                                              rF_[0:1, 0:W],
                                              start=True, stop=True)), "pe")
                    pbr[7].note_wr("pe", pe)
                    rows_r2.note_rd("pe", pe)
                    prep(0)
                    for j in range(NA):
                        body(j)
                xi_r.wr("dv")
                dv = p.op("dv", lambda e: e.memset(
                    bass.AP(xi, NCOMP, [[4 * COLS, 128], [COLS, 4],
                                        [1, COLS - NCOMP]]), 0.0), "dv")
                xi_r.note_wr("dv", dv)

                for t in ("y1", "xi"):
                    if t in taps:
                        src, res, nch = ((y1, y1_r, 8) if t == "y1"
                                         else (xi, xi_r, 4))
                        res.rd("gp")
                        st = p.op("gp", (lambda e, src=src, nch=nch, t=t:
                                         e.dma_start(
                                             bass.AP(taps[t], 0,
                                                     [[COLS, 128],
                                                      [128 * COLS, nch],
                                                      [1, COLS]]),
                                             bass.AP(src, 0,
                                                     [[nch * COLS, 128],
                                                      [COLS, nch],
                                                      [1, COLS]]))), "st", 16)
                        res.note_rd("st", st)
            p.fence()

            with contextlib.ExitStack() as bx:
                rows = sb("rows_s", [1, 512], bx)
                ident = sb("ident_s", [128, 128], bx)
                p.op("sp", dma(rows[:, :], rows_h[:, :]), "ld", 16)
                p.op("sp", dma(ident[:, :], ident_h[:, :]), "ld", 16)
                wqk = [sb(f"wqk{b_}", [128, 1024], bx) for b_ in (0, 1)]
                wv = [sb(f"wv{b_}", [128, 512], bx) for b_ in (0, 1)]
                wpj = [sb(f"wpj{b_}", [128, 512], bx) for b_ in (0, 1)]
                tabs = {(bi, cls): sb(f"tab{bi}{cls}", [128, 1024], bx)
                        for bi in (0, 1) for cls in (0, 1, 2)}
                npre_b = 8 * NBLK + 2 * 128 * 256 + 4 * NBLK
                for bi in ((0, 1) if "B" in PASSES else ()):
                    off = bi * npre_b
                    p.op("sp", dma(wqk[bi][:, :], bass.AP(
                        pre_h, off, [[128, 128], [NBLK, 8], [1, 128]])),
                        "ld", 16)
                    p.op("sp", dma(wv[bi][:, :], bass.AP(
                        pre_h, off + 8 * NBLK,
                        [[256, 128], [128 * 256, 2], [1, 256]])), "ld", 16)
                    p.op("sp", dma(wpj[bi][:, :], bass.AP(
                        pre_h, off + 8 * NBLK + 2 * 128 * 256,
                        [[128, 128], [NBLK, 4], [1, 128]])), "ld", 16)
                toff = 0
                ldB = None
                for bi in ((0, 1) if "B" in PASSES else ()):
                    for cls in (0, 1, 2):
                        C = CLS_W[bi][cls]
                        ldB = p.op("sp", dma(
                            bass.AP(tabs[(bi, cls)], 0,
                                    [[1024, C], [128, 8], [1, C]]),
                            bass.AP(atab_h, toff,
                                    [[C, C], [C * C, 8], [1, C]])), "ld", 16)
                        toff += 8 * C * C
                p.wait("pe", "ld", ldB)
                p.wait("dv", "ld", ldB)

                qk_sb = [sb(f"qk{i}", [128, 512], bx) for i in range(2)]
                vT_sb = [sb(f"vT{i}", [128, 256], bx) for i in range(4)]
                S_sb = [sb(f"S{i}", [128, 1024], bx) for i in range(2)]
                P_sb = [sb(f"P{i}", [128, 1024], bx) for i in range(2)]
                PT_sb = [sb(f"PT{i}", [128, 1024], bx) for i in range(2)]
                O_sb = [sb(f"O{i}", [128, 256], bx) for i in range(2)]
                nmx = [sb(f"nmx{i}", [128, 8], bx) for i in range(2)]
                sms = [sb(f"sms{i}", [128, 8], bx) for i in range(2)]
                rcp = [sb(f"rcp{i}", [128, 8], bx) for i in range(2)]
                RR = lambda: [Res(p), Res(p)]
                qk_r, S_r, P_r = RR(), RR(), RR()
                vT_r = [Res(p) for _ in range(4)]
                PT_r, O_r, nmx_r, sms_r, rcp_r = RR(), RR(), RR(), RR(), RR()

                ci_all = [(bi, s, C, cls) for bi in (0, 1)
                          for (s, C, cls) in CHUNKS[bi]] if "B" in PASSES else []

                def b_qkvT(idx):
                    bi, s, C, cls = ci_all[idx]
                    par = idx % 2
                    pv3 = idx % 4
                    bqk = VECS[f"qkv{bi+1}_b_qk"][0]
                    pbr[0].wr("pe")
                    xi_r.rd("pe")
                    for m in range(4):
                        for k in range(2):
                            pe = p.op("pe", (lambda e, m=m, k=k, bi=bi, s=s,
                                             C=C: e.matmul(
                                                 pbk[0][:, m * 128:m * 128 + C],
                                                 wqk[bi][:, (m * 2 + k) * 128:
                                                         (m * 2 + k + 1) * 128],
                                                 xi[:, (bi * 2 + k) * COLS + s:
                                                    (bi * 2 + k) * COLS + s + C],
                                                 start=(k == 0), stop=(k == 1))),
                                      "pe" if k == 1 else None)
                    pbr[0].note_wr("pe", pe)
                    qk_r[par].wr("dv")
                    pbr[0].rd("dv")
                    for m in range(4):
                        dv = p.op("dv", (lambda e, m=m, C=C, par=par,
                                         bqk=bqk: e.tensor_scalar_add(
                                             qk_sb[par][:, m * 128:m * 128 + C],
                                             pbk[0][:, m * 128:m * 128 + C],
                                             vecs[:, bqk + m:bqk + m + 1])),
                                  "dv")
                    pbr[0].note_rd("dv", dv)
                    qk_r[par].note_wr("dv", dv)
                    pbr[1].wr("pe")
                    for k in range(2):
                        p.op("pe", (lambda e, k=k, bi=bi, s=s, C=C: e.matmul(
                            pbk[1][0:C, 0:256],
                            xi[:, (bi * 2 + k) * COLS + s:
                               (bi * 2 + k) * COLS + s + C],
                            wv[bi][:, k * 256:(k + 1) * 256],
                            start=(k == 0), stop=False)))
                    pe = p.op("pe", (lambda e, bi=bi, C=C: e.matmul(
                        pbk[1][0:C, 0:256], ones1[0:1, 0:C],
                        rows[0:1, bi * 256:(bi + 1) * 256],
                        start=False, stop=True)), "pe")
                    pbr[1].note_wr("pe", pe)
                    xi_r.note_rd("pe", pe)
                    vT_r[pv3].wr("dv")
                    pbr[1].rd("dv")
                    dv = p.op("dv", (lambda e, C=C, pv3=pv3: e.tensor_copy(
                        vT_sb[pv3][0:C, 0:256], pbk[1][0:C, 0:256])), "dv")
                    pbr[1].note_rd("dv", dv)
                    vT_r[pv3].note_wr("dv", dv)

                def b_scores(idx):
                    bi, s, C, cls = ci_all[idx]
                    par = idx % 2
                    for b_ in (2, 3, 4, 5):
                        pbr[b_].wr("pe")
                    qk_r[par].rd("pe")
                    for h in range(8):
                        bk = 2 + h % 4
                        rg = 32 * (h % 4)
                        sc = (h // 4) * 128
                        pe = p.op("pe", (lambda e, h=h, bk=bk, rg=rg, sc=sc,
                                         C=C, par=par: e.matmul(
                                             pbk[bk][0:C, sc:sc + C],
                                             qk_sb[par][rg:rg + 32,
                                                        (h // 4) * 128:
                                                        (h // 4) * 128 + C],
                                             qk_sb[par][rg:rg + 32,
                                                        (2 + h // 4) * 128:
                                                        (2 + h // 4) * 128 + C],
                                             start=True, stop=True,
                                             tile_position=(rg, 0))),
                                  "pe" if h == 7 else None)
                        if h == 7:
                            for b_ in (2, 3, 4, 5):
                                pbr[b_].note_wr("pe", pe)
                            qk_r[par].note_rd("pe", pe)
                    S_r[par].wr("dv")
                    tab = tabs[(bi, cls)]
                    for h in range(8):
                        bk = 2 + h % 4
                        sc = (h // 4) * 128
                        pbr[bk].rd("dv")
                        dv = p.op("dv", (lambda e, h=h, bk=bk, sc=sc, C=C,
                                         par=par, tab=tab:
                                         e.scalar_tensor_tensor(
                                             S_sb[par][0:C, h * 128:h * 128 + C],
                                             pbk[bk][0:C, sc:sc + C],
                                             SCALE,
                                             tab[0:C, h * 128:h * 128 + C],
                                             ALU.mult, ALU.add)), "dv")
                        pbr[bk].note_rd("dv", dv)
                    S_r[par].note_wr("dv", dv)
                    nmx_r[par].wr("dv")
                    dv = p.op("dv", (lambda e, C=C, par=par: e.tensor_reduce(
                        nmx[par][0:C, 0:8],
                        bass.AP(S_sb[par], 0, [[1024, C], [128, 8], [1, C]]),
                        AX.X, ALU.max, negate=True)), "dv")
                    nmx_r[par].note_wr("dv", dv)
                    P_r[par].wr("ac")
                    sms_r[par].wr("ac")
                    S_r[par].rd("ac")
                    nmx_r[par].rd("ac")
                    for h in range(8):
                        ac = p.op("ac", (lambda e, h=h, C=C, par=par:
                                         e.activation(
                                             P_sb[par][0:C, h * 128:h * 128 + C],
                                             S_sb[par][0:C, h * 128:h * 128 + C],
                                             AF.Exp,
                                             bias=nmx[par][0:C, h:h + 1],
                                             scale=1.0,
                                             accum_out=sms[par][0:C, h:h + 1])),
                                  "ac")
                    P_r[par].note_wr("ac", ac)
                    sms_r[par].note_wr("ac", ac)
                    S_r[par].note_rd("ac", ac)
                    nmx_r[par].note_rd("ac", ac)

                def b_pnorm(idx):
                    bi, s, C, cls = ci_all[idx]
                    par = idx % 2
                    rcp_r[par].wr("dv")
                    sms_r[par].rd("dv")
                    dv = p.op("dv", (lambda e, C=C, par=par: e.reciprocal(
                        rcp[par][0:C, 0:8], sms[par][0:C, 0:8])), "dv")
                    rcp_r[par].note_wr("dv", dv)
                    sms_r[par].note_rd("dv", dv)
                    p.flush("dv")
                    P_r[par].rd("dv")
                    for h in range(8):
                        dv = p.op("dv", (lambda e, h=h, C=C, par=par:
                                         e.tensor_scalar_mul(
                                             P_sb[par][0:C, h * 128:h * 128 + C],
                                             P_sb[par][0:C, h * 128:h * 128 + C],
                                             rcp[par][0:C, h:h + 1])), "dv")
                    P_r[par].note_wr("dv", dv)
                    rcp_r[par].note_rd("dv", dv)

                def b_back2a(idx):
                    bi, s, C, cls = ci_all[idx]
                    par = idx % 2
                    pbr[6].wr("pe")
                    pbr[7].wr("pe")
                    P_r[par].rd("pe")
                    for h in range(8):
                        bk = 6 + h // 4
                        pe = p.op("pe", (lambda e, h=h, bk=bk, C=C, par=par:
                                         e.transpose(
                                             pbk[bk][0:C, (h % 4) * 128:
                                                     (h % 4) * 128 + C],
                                             P_sb[par][0:C, h * 128:h * 128 + C],
                                             ident[0:C, 0:C])),
                                  "pe" if h in (3, 7) else None)
                        if h in (3, 7):
                            pbr[bk].note_wr("pe", pe)
                        if h == 7:
                            P_r[par].note_rd("pe", pe)
                    PT_r[par].wr("dv")
                    for g in (0, 1):
                        bk = 6 + g
                        pbr[bk].rd("dv")
                        dv = p.op("dv", (lambda e, g=g, bk=bk, C=C, par=par:
                                         e.tensor_copy(
                                             bass.AP(PT_sb[par], g * 512,
                                                     [[1024, C], [128, 4],
                                                      [1, C]]),
                                             bass.AP(pbk[bk], 0,
                                                     [[512, C], [128, 4],
                                                      [1, C]]))), "dv")
                        pbr[bk].note_rd("dv", dv)
                    PT_r[par].note_wr("dv", dv)

                def b_back2b(idx):
                    bi, s, C, cls = ci_all[idx]
                    par = idx % 2
                    pv3 = idx % 4
                    pjb = VECS[f"proj{bi+1}_b"][0]
                    pbr[6].wr("pe")
                    vT_r[pv3].rd("pe")
                    PT_r[par].rd("pe")
                    for h in range(8):
                        cg = 32 * (h % 4)
                        pe = p.op("pe", (lambda e, h=h, cg=cg, C=C, par=par,
                                         pv3=pv3: e.matmul(
                                             pbk[6][cg:cg + 32,
                                                    (h // 4) * 128:
                                                    (h // 4) * 128 + C],
                                             vT_sb[pv3][0:C, h * 32:h * 32 + 32],
                                             PT_sb[par][0:C,
                                                        h * 128:h * 128 + C],
                                             start=True, stop=True,
                                             tile_position=(0, cg))),
                                  "pe" if h == 7 else None)
                    pbr[6].note_wr("pe", pe)
                    vT_r[pv3].note_rd("pe", pe)
                    PT_r[par].note_rd("pe", pe)
                    O_r[par].wr("dv")
                    pbr[6].rd("dv")
                    for g in (0, 1):
                        dv = p.op("dv", (lambda e, g=g, C=C, par=par:
                                         e.tensor_copy(
                                             O_sb[par][:, g * 128:g * 128 + C],
                                             pbk[6][:, g * 128:g * 128 + C])),
                                  "dv")
                    pbr[6].note_rd("dv", dv)
                    O_r[par].note_wr("dv", dv)
                    pbr[7].wr("pe")
                    O_r[par].rd("pe")
                    for m in range(2):
                        for k in range(2):
                            pe = p.op("pe", (lambda e, m=m, k=k, bi=bi, C=C,
                                             par=par: e.matmul(
                                                 pbk[7][:, m * 128:m * 128 + C],
                                                 wpj[bi][:, (m * 2 + k) * 128:
                                                         (m * 2 + k + 1) * 128],
                                                 O_sb[par][:, k * 128:
                                                           k * 128 + C],
                                                 start=(k == 0), stop=(k == 1))),
                                      "pe" if k == 1 else None)
                    pbr[7].note_wr("pe", pe)
                    O_r[par].note_rd("pe", pe)
                    o_r.wr("dv")
                    pbr[7].rd("dv")
                    for m in range(2):
                        dv = p.op("dv", (lambda e, m=m, bi=bi, s=s, C=C,
                                         pjb=pjb: e.tensor_scalar_add(
                                             o_full[:, (bi * 2 + m) * COLS + s:
                                                    (bi * 2 + m) * COLS + s + C],
                                             pbk[7][:, m * 128:m * 128 + C],
                                             vecs[:, pjb + m:pjb + m + 1])),
                                  "dv")
                    pbr[7].note_rd("dv", dv)
                    o_r.note_wr("dv", dv)
                    if idx == BDUMP:
                        for nm_, buf, res in (("qk", qk_sb[par], qk_r[par]),
                                              ("vT", vT_sb[par], vT_r[par]),
                                              ("S", S_sb[par], S_r[par]),
                                              ("P", P_sb[par], P_r[par]),
                                              ("O", O_sb[par], O_r[par]),
                                              ("nmx", nmx[par], nmx_r[par]),
                                              ("sms", sms[par], sms_r[par])):
                            res.rd("gp")
                            st = p.op("gp", (lambda e, buf=buf, nm_=nm_:
                                             e.dma_start(
                                                 bdump_h[nm_][:, :],
                                                 buf[:, :])), "st", 16)
                            res.note_rd("st", st)

                if ci_all:
                    b_qkvT(0)
                    if len(ci_all) > 1:
                        b_qkvT(1)
                    b_scores(0)
                for i in range(len(ci_all)):
                    b_pnorm(i)
                    if i + 2 < len(ci_all):
                        b_qkvT(i + 2)
                    b_back2a(i)
                    if i + 1 < len(ci_all):
                        b_scores(i + 1)
                    b_back2b(i)
                if "o" in taps:
                    o_r.rd("gp")
                    st = p.op("gp", lambda e: e.dma_start(
                        bass.AP(taps["o"], 0, [[COLS, 128], [128 * COLS, 4],
                                               [1, COLS]]),
                        bass.AP(o_full, 0, [[4 * COLS, 128], [COLS, 4],
                                            [1, COLS]])), "st", 16)
                    o_r.note_rd("st", st)
            p.fence()
        p.fence()

        with contextlib.ExitStack() as cx:
            ring = Ring(p, sb("ringC", [128, 48 * 128], cx, BF16),
                        wsplit_h, split_ready)
            s1 = sb("s1C", [128, 8 * CW], cx)
            s2 = sb("s2C", [128, 8 * CW], cx)
            s2z = sb("s2zC", [128, 8 * CW], cx)
            s3 = sb("s3C", [128, 8 * CW], cx)
            s5 = sb("s5C", [128, 8 * CW], cx)
            pairC_hi = sb("pairC_hi", [128, 8 * CW], cx, BF16)
            pairC_lo = sb("pairC_lo", [128, 8 * CW], cx, BF16)
            pairC_r = Res(p)
            hidC_hi = sb("hidC_hi", [128, 16 * CW], cx, BF16)
            hidC_lo = sb("hidC_lo", [128, 16 * CW], cx, BF16)
            hid_r = Res(p)
            sqCs = [sb(f"sqC{i}", [128, CW], cx) for i in range(2)]
            tC, tC_r = sqCs, None
            rAc = sb("rAC", [1, CW], cx)
            rBc = sb("rBC", [1, CW], cx)
            rCc = sb("rCC", [1, CW], cx)
            s1_r, s2_r, s2z_r, s3_r, s5_r = (Res(p) for _ in range(5))
            sq_r2s = [Res(p), Res(p)]
            tC_r = sq_r2s
            rows_r2 = Res(p)

            def cst(j):
                z0 = 19 if j == 0 else CTW * j - 1
                ZW = min(NCOMP, CTW * (j + 1)) - z0
                lc0 = 1 if j == 0 else 0
                return z0, ZW, lc0, ZW - lc0, ZW - 1

            def c_head(j):
                z0, ZW, lc0, CWj, OWj = cst(j)
                rows_r2.wr("sp")
                ldv = p.op("sp", (lambda e, z0=z0, lc0=lc0, CWj=CWj:
                                  e.dma_start(rBc[0:1, 0:CWj],
                                              vstat_h[0:1, z0 + lc0:
                                                      z0 + lc0 + CWj])),
                           "ld", 16)
                ldv = p.op("sp", (lambda e, z0=z0, lc0=lc0, CWj=CWj:
                                  e.dma_start(rCc[0:1, 0:CWj],
                                              vstat_h[1:2, z0 + lc0:
                                                      z0 + lc0 + CWj])),
                           "ld", 16)
                rows_r2.note_wr("ld", ldv)
                pbr[4].wr("pe")
                pbr[5].wr("pe")
                rows_r2.rd("pe")
                pe = p.op("pe", (lambda e, CWj=CWj: e.matmul(
                    pbk[4][:, 0:CWj], ones1[0:1, :], rBc[0:1, 0:CWj],
                    start=True, stop=True)))
                pe = p.op("pe", (lambda e, CWj=CWj: e.matmul(
                    pbk[5][:, 0:CWj], ones1[0:1, :], rCc[0:1, 0:CWj],
                    start=True, stop=True)), "pe")
                pbr[4].note_wr("pe", pe)
                pbr[5].note_wr("pe", pe)
                rows_r2.note_rd("pe", pe)
                normalize(y1, y1_r, s1, s1_r, 8, COLS, z0 + lc0, CW, lc0, CWj)
                split_pair(o_full, o_r, pairC_hi, pairC_lo, pairC_r,
                           4, COLS, z0 + lc0, CW, lc0, CWj)

            def c_up(j):
                z0, ZW, lc0, CWj, OWj = cst(j)
                s2_r.wr("dv")

                def ev_w1s(m, bk):
                    pbr[bk].rd("dv")
                    s1_r.rd("dv")
                    dv = p.op("dv", (lambda e, m=m, bk=bk, lc0=lc0, CWj=CWj:
                                     e.scalar_tensor_tensor(
                                         s2[:, m * CW + lc0: m * CW + lc0 + CWj],
                                         pbk[bk][:, 0:CWj],
                                         vecs[:, VECS["up_b"][0] + m:
                                              VECS["up_b"][0] + m + 1],
                                         s1[:, m * CW + lc0:
                                            m * CW + lc0 + CWj],
                                         ALU.add, ALU.add)), "dv")
                    pbr[bk].note_rd("dv", dv)
                    if m == 7:
                        s2_r.note_wr("dv", dv)
                        s1_r.note_rd("dv", dv)
                mm_block(ring, OFFS["up"], 8, 4,
                         lambda k: pairC_hi[:, k * CW + lc0: k * CW + lc0 + CWj],
                         lambda k: pairC_lo[:, k * CW + lc0: k * CW + lc0 + CWj],
                         pairC_r, CWj, ev_w1s)

            def c_stats_w1s(j):
                z0, ZW, lc0, CWj, OWj = cst(j)
                return stats(s2, s2_r, 8, CW, lc0, CWj, inv1024[:, :],
                             EPS6_COL, False, sqCs, sq_r2s, rAc, rBc, rCc,
                             rows_r2, bks=(6, 7, 4, 5), defer_bcast=True)

            def c_norm2(j):
                z0, ZW, lc0, CWj, OWj = cst(j)
                s3_r.wr("dv")
                pbr[4].rd("dv")
                for m in range(8):
                    dv = p.op("dv", (lambda e, m=m, lc0=lc0, CWj=CWj:
                                     e.scalar_tensor_tensor(
                                         s3[:, m * CW + lc0: m * CW + lc0 + CWj],
                                         s2[:, m * CW + lc0: m * CW + lc0 + CWj],
                                         vecs[:, VECS["norm2_w"][0] + m:
                                              VECS["norm2_w"][0] + m + 1],
                                         pbk[4][:, 0:CWj], ALU.mult,
                                         ALU.mult)), "dv")
                s3_r.note_wr("dv", dv)
                pbr[4].note_rd("dv", dv)
                split_pair(s3, s3_r, pairC_hi, pairC_lo, pairC_r,
                           8, CW, lc0, CW, lc0, CWj)

            def c_ffn1(j):
                z0, ZW, lc0, CWj, OWj = cst(j)

                def ev_gelu(m, bk):
                    sl = m % 2
                    tC_r[sl].wr("ac")
                    pbr[bk].rd("ac")
                    ac = p.op("ac", (lambda e, m=m, bk=bk, sl=sl, lc0=lc0,
                                     CWj=CWj: e.activation(
                                         tC[sl][:, 0:CWj],
                                         pbk[bk][:, 0:CWj], AF.Gelu,
                                         bias=vecs[:, VECS["ffn_b1"][0] + m:
                                                   VECS["ffn_b1"][0] + m + 1],
                                         scale=1.0)), "ac")
                    pbr[bk].note_rd("ac", ac)
                    tC_r[sl].note_wr("ac", ac)
                    if m == 0:
                        hid_r.wr("dv")
                    tC_r[sl].rd("dv")
                    hp = hidC_hi[:, m * CW + lc0: m * CW + lc0 + CWj]
                    lp = hidC_lo[:, m * CW + lc0: m * CW + lc0 + CWj]
                    p.op("dv", (lambda e, sl=sl, hp=hp, CWj=CWj:
                                e.tensor_copy(hp, tC[sl][:, 0:CWj])))
                    dv = p.op("dv", (lambda e, sl=sl, hp=hp, lp=lp, CWj=CWj:
                                     e.tensor_tensor(lp, tC[sl][:, 0:CWj], hp,
                                                     ALU.subtract)), "dv")
                    tC_r[sl].note_rd("dv", dv)
                    if m == 15:
                        hid_r.note_wr("dv", dv)
                mm_block(ring, OFFS["ffn1"], 16, 8,
                         lambda k: pairC_hi[:, k * CW + lc0: k * CW + lc0 + CWj],
                         lambda k: pairC_lo[:, k * CW + lc0: k * CW + lc0 + CWj],
                         pairC_r, CWj, ev_gelu)

            def c_ffn2(j):
                z0, ZW, lc0, CWj, OWj = cst(j)
                s5_r.wr("dv")
                for m in range(8):
                    bk = m % 4
                    pbr[bk].wr("pe")
                    hid_r.rd("pe")
                    for half in range(2):
                        ring.load_pair_group(
                            OFFS["ffn2"] + (m * 16 + half * 8) * NBLK, 8)
                        s0g = ring.use_group()
                        for kk in range(8):
                            k = half * 8 + kk
                            whi = ring.blk(s0g, 2 * kk)
                            wlo = ring.blk(s0g, 2 * kk + 1)
                            ahi = hidC_hi[:, k * CW + lc0: k * CW + lc0 + CWj]
                            alo = hidC_lo[:, k * CW + lc0: k * CW + lc0 + CWj]
                            for i3, (wap, aap) in enumerate(
                                    ((whi, ahi), (whi, alo), (wlo, ahi))):
                                pe = p.op("pe", (lambda e, wap=wap, aap=aap,
                                                 bk=bk, CWj=CWj,
                                                 st=(k == 0 and i3 == 0),
                                                 sp=(k == 15 and i3 == 2):
                                                 e.matmul(pbk[bk][:, 0:CWj],
                                                          wap, aap,
                                                          start=st, stop=sp)),
                                          "pe" if (kk == 7 and i3 == 2)
                                          else None)
                        ring.done_group(pe)
                    pbr[bk].note_wr("pe", pe)
                    if m == 7:
                        hid_r.note_rd("pe", pe)
                    pbr[bk].rd("dv")
                    s2_r.rd("dv")
                    y1_r.rd("dv")
                    p.op("dv", (lambda e, m=m, bk=bk, lc0=lc0, CWj=CWj:
                                e.scalar_tensor_tensor(
                                    s5[:, m * CW + lc0: m * CW + lc0 + CWj],
                                    pbk[bk][:, 0:CWj],
                                    vecs[:, VECS["ffn_b2"][0] + m:
                                         VECS["ffn_b2"][0] + m + 1],
                                    s2[:, m * CW + lc0: m * CW + lc0 + CWj],
                                    ALU.add, ALU.add)))
                    dv = p.op("dv", (lambda e, m=m, z0=z0, lc0=lc0, CWj=CWj:
                                     e.tensor_tensor(
                                         s5[:, m * CW + lc0: m * CW + lc0 + CWj],
                                         s5[:, m * CW + lc0: m * CW + lc0 + CWj],
                                         y1[:, m * COLS + z0 + lc0:
                                            m * COLS + z0 + lc0 + CWj],
                                         ALU.add)), "dv")
                    pbr[bk].note_rd("dv", dv)
                    if m == 7:
                        s2_r.note_rd("dv", dv)
                        y1_r.note_rd("dv", dv)
                s5_r.note_wr("dv", dv)
                if j == 0:
                    dv = p.op("dv", lambda e: e.memset(
                        bass.AP(s5, 0, [[8 * CW, 128], [CW, 8], [1, 1]]),
                        0.0), "dv")
                    s5_r.note_wr("dv", dv)

            def c_stats_ln3(j):
                z0, ZW, lc0, CWj, OWj = cst(j)
                return stats(s5, s5_r, 8, CW, 0, ZW, inv1024[:, :], EPS5_COL,
                             True, sqCs, sq_r2s, rAc, rBc, rCc, rows_r2,
                             bks=(6, 7, 4, 5), defer_bcast=True)

            def c_norm3r(j):
                z0, ZW, lc0, CWj, OWj = cst(j)
                normalize(s5, s5_r, s2z, s2z_r, 8, CW, 0, CW, 0, ZW)
                shiftmix(s2z, s2z_r, s3, s3_r, CW, OWj, VECS["mu_r"][0])
                split_pair(s3, s3_r, pairC_hi, pairC_lo, pairC_r,
                           8, CW, 1, CW, 1, OWj)

            def c_wr(j):
                z0, ZW, lc0, CWj, OWj = cst(j)
                s1_r.wr("ac")

                def ev_sig2(m, bk):
                    pbr[bk].rd("ac")
                    ac = p.op("ac", (lambda e, m=m, bk=bk, OWj=OWj:
                                     e.activation(
                                         s1[:, m * CW + 1: m * CW + 1 + OWj],
                                         pbk[bk][:, 0:OWj], AF.Sigmoid,
                                         bias=0.0, scale=1.0)), "ac")
                    pbr[bk].note_rd("ac", ac)
                    if m == 7:
                        s1_r.note_wr("ac", ac)
                mm_block(ring, OFFS["Wr"], 8, 8,
                         lambda k: pairC_hi[:, k * CW + 1: k * CW + 1 + OWj],
                         lambda k: pairC_lo[:, k * CW + 1: k * CW + 1 + OWj],
                         pairC_r, OWj, ev_sig2)

            def c_shiftk(j):
                z0, ZW, lc0, CWj, OWj = cst(j)
                shiftmix(s2z, s2z_r, s3, s3_r, CW, OWj, VECS["mu_k"][0])
                split_pair(s3, s3_r, pairC_hi, pairC_lo, pairC_r,
                           8, CW, 1, CW, 1, OWj)

            def c_wk(j):
                z0, ZW, lc0, CWj, OWj = cst(j)
                hid_r.wr("dv")
                hid_r.wr("ac")

                def ev_relu2b(m, bk):
                    sl = m % 2
                    tC_r[sl].wr("ac")
                    pbr[bk].rd("ac")
                    p.op("ac", (lambda e, bk=bk, sl=sl, OWj=OWj:
                                e.activation(tC[sl][:, 0:OWj],
                                             pbk[bk][:, 0:OWj], AF.Relu,
                                             bias=0.0, scale=1.0)), "ac")
                    ac = p.op("ac", (lambda e, sl=sl, OWj=OWj: e.activation(
                        tC[sl][:, 0:OWj], tC[sl][:, 0:OWj],
                        AF.Square, bias=0.0, scale=1.0)), "ac")
                    pbr[bk].note_rd("ac", ac)
                    tC_r[sl].note_wr("ac", ac)
                    p.wait("dv", "ac", ac)
                    hp = hidC_hi[:, m * CW + 1: m * CW + 1 + OWj]
                    lp = hidC_lo[:, m * CW + 1: m * CW + 1 + OWj]
                    p.op("dv", (lambda e, sl=sl, hp=hp, OWj=OWj:
                                e.tensor_copy(hp, tC[sl][:, 0:OWj])))
                    dv = p.op("dv", (lambda e, sl=sl, hp=hp, lp=lp, OWj=OWj:
                                     e.tensor_tensor(lp, tC[sl][:, 0:OWj], hp,
                                                     ALU.subtract)), "dv")
                    tC_r[sl].note_rd("dv", dv)
                    if m == 7:
                        hid_r.note_wr("dv", dv)
                mm_block(ring, OFFS["Wk"], 8, 8,
                         lambda k: pairC_hi[:, k * CW + 1: k * CW + 1 + OWj],
                         lambda k: pairC_lo[:, k * CW + 1: k * CW + 1 + OWj],
                         pairC_r, OWj, ev_relu2b)

            def c_wv(j):
                z0, ZW, lc0, CWj, OWj = cst(j)
                s5_r.wr("dv")
                s1_r.rd("dv")

                def ev_out(m, bk):
                    sl = m % 2
                    tC_r[sl].wr("dv")
                    pbr[bk].rd("dv")
                    p.op("dv", (lambda e, m=m, bk=bk, sl=sl, OWj=OWj:
                                e.scalar_tensor_tensor(
                                    tC[sl][:, 0:OWj],
                                    pbk[bk][:, 0:OWj], 0.5,
                                    s1[:, m * CW + 1: m * CW + 1 + OWj],
                                    ALU.mult, ALU.mult)))
                    dv = p.op("dv", (lambda e, m=m, sl=sl, OWj=OWj:
                                     e.tensor_tensor(
                                         s5[:, m * CW + 1: m * CW + 1 + OWj],
                                         s5[:, m * CW + 1: m * CW + 1 + OWj],
                                         tC[sl][:, 0:OWj], ALU.add)), "dv")
                    pbr[bk].note_rd("dv", dv)
                    tC_r[sl].note_wr("dv", dv)
                    tC_r[sl].note_rd("dv", dv)
                    if m == 7:
                        s1_r.note_rd("dv", dv)
                        s5_r.note_wr("dv", dv)
                mm_block(ring, OFFS["Wv"], 8, 8,
                         lambda k: hidC_hi[:, k * CW + 1: k * CW + 1 + OWj],
                         lambda k: hidC_lo[:, k * CW + 1: k * CW + 1 + OWj],
                         hid_r, OWj, ev_out)

                s5_r.rd("gp")
                st = p.op("gp", (lambda e, z0=z0, OWj=OWj: e.dma_start(
                    bass.AP(out_h, z0 + 1 - OUT0,
                            [[OUTW, 128], [128 * OUTW, 8], [1, OWj]]),
                    bass.AP(s5, 1, [[8 * CW, 128], [CW, 8], [1, OWj]]))),
                    "st", 16)
                s5_r.note_rd("st", st)

            NJ = NCT if "C" in PASSES else 0
            bcW, bcZ = {}, {}
            if NJ:
                c_head(0)
                c_up(0)
                bcW[0] = c_stats_w1s(0)
                bcW[0]()
                c_norm2(0)
            for j in range(NJ):
                c_ffn1(j)
                if j + 1 < NJ:
                    c_head(j + 1)
                c_ffn2(j)
                bcZ[j] = c_stats_ln3(j)
                if j + 1 < NJ:
                    c_up(j + 1)
                bcZ[j]()
                if j + 1 < NJ:
                    bcW[j + 1] = c_stats_w1s(j + 1)
                c_norm3r(j)
                c_wr(j)
                if j + 1 < NJ:
                    bcW[j + 1]()
                c_shiftk(j)
                c_wk(j)
                if j + 1 < NJ:
                    c_norm2(j + 1)
                c_wv(j)

            p.q["gp"].append(("w", "st", p.cnt["st"]))

        with nc.Block() as block:
            p.emit(block)
    return nc


def kernel(**inputs):
    in_maps = _host_pack(inputs)
    nc = build(DEBUG_TAPS)
    res = run_bass_kernel_spmd(nc, in_maps, core_ids=list(range(NCORE)))
    return _host_unpack(res.results)
